# version 23
# baseline (speedup 1.0000x reference)
import math
from contextlib import ExitStack

import numpy as np

import concourse.bass as bass
import concourse.mybir as mybir
from concourse.bass_utils import run_bass_kernel_spmd
from concourse.bass_core_rust import IndirectOffsetOnAxis
from concourse import library_config

F32 = mybir.dt.float32
BF16 = mybir.dt.bfloat16
U32 = mybir.dt.uint32
I32 = mybir.dt.int32
I16 = mybir.dt.int16
AF = mybir.ActivationFunctionType
ALU = mybir.AluOpType
AX = mybir.AxisListType


ENGS = ("t", "v", "s", "g", "q")
DMA_CHAINS = ("dq", "dg", "ds")
PSUM_BANK = 2048


def _ap_extent(ap):
    t = ap.tensor
    esz = mybir.dt.size(ap.dtype)
    dims = list(ap.ap)
    off = ap.offset if isinstance(ap.offset, int) else 0
    from concourse.bass import MemorySpace
    sp = ap.space
    if sp in (MemorySpace.SBUF, MemorySpace.PSUM):
        base_p = getattr(t, "base_partition", 0) or 0
        pstep, pcount = dims[0]
        if pstep > 0:
            p_off, col = off // pstep, off % pstep
        else:
            p_off, col = 0, off
        plo, phi = base_p + p_off, base_p + p_off + pcount
        lo, hi = 0, 1
        for step, count in dims[1:]:
            if step >= 0:
                hi += step * (count - 1)
            else:
                lo += step * (count - 1)
        blo, bhi = (col + lo) * esz, (col + hi) * esz
        if sp == MemorySpace.PSUM:
            blo = (blo // PSUM_BANK) * PSUM_BANK
            bhi = -(-bhi // PSUM_BANK) * PSUM_BANK
            plo, phi = 0, 128
        return (t.name, plo, phi, blo, bhi)
    lo, hi = 0, 1
    for step, count in dims:
        if step >= 0:
            hi += step * (count - 1)
        else:
            lo += step * (count - 1)
    return (t.name, 0, 128, (off + lo) * esz, (off + hi) * esz)


def _overlap(a, b):
    return (a[0] == b[0] and a[1] < b[2] and b[1] < a[2]
            and a[3] < b[4] and b[3] < a[4])


def _covers(a, b):
    return (a[0] == b[0] and a[1] <= b[1] and a[2] >= b[2]
            and a[3] <= b[3] and a[4] >= b[4])


class K:
    def __init__(self, nc, stack: ExitStack):
        self.nc = nc
        self.stack = stack
        self.chain = {}
        self.cnt = {}
        for e in ENGS + DMA_CHAINS:
            self.new_chain(e)
        self.obs = {e: {} for e in ENGS}
        self.clock_at = {}
        self.writes = {}
        self.reads = {}
        self.prog = {e: [] for e in ENGS}
        self.last_tsig = None

    def new_chain(self, name):
        if name not in self.chain:
            self.chain[name] = self.stack.enter_context(
                self.nc.semaphore(f"chain_{name}"))
            self.cnt[name] = 0
        return name

    def _need_waits(self, e, raps, waps):
        need = {}
        for ap in raps:
            ext = _ap_extent(ap)
            for (wext, ch, c) in self.writes.get(ext[0], ()):
                if _overlap(ext, wext):
                    need[ch] = max(need.get(ch, 0), c)
        for ap in waps:
            ext = _ap_extent(ap)
            for (wext, ch, c) in self.writes.get(ext[0], ()):
                if _overlap(ext, wext):
                    need[ch] = max(need.get(ch, 0), c)
            for (rext, ch, c) in self.reads.get(ext[0], ()):
                if _overlap(ext, rext):
                    need[ch] = max(need.get(ch, 0), c)
        out = {}
        for ch, c in need.items():
            if ch == e and c > self.cnt[e]:
                continue
            out[ch] = c
        return out

    def _observe(self, e, ch, c):
        obs = self.obs[e]
        if obs.get(ch, 0) >= c:
            return
        obs[ch] = c
        snap = self.clock_at.get((ch, c))
        if snap:
            for ch2, c2 in snap.items():
                if obs.get(ch2, 0) < c2:
                    obs[ch2] = c2

    def op(self, e, emit, reads=(), writes=(), inc=True, dma=False,
           dma_chain=None, extra_waits=None, chain=None, step=None,
           serial=False, sig=None):
        if e == "t" and sig is not None:
            if self.last_tsig is not None and sig != self.last_tsig:
                self.prog["t"].append(({}, lambda eng: eng.drain(), False, 0, "t"))
            self.last_tsig = sig
        if dma:
            chain, step, serial = dma_chain, 16, True
        ch = chain if chain is not None else e
        step = step if step is not None else 1
        waits = self._need_waits(e, reads, writes)
        if serial and self.cnt[ch] > 0:
            waits[ch] = max(waits.get(ch, 0), self.cnt[ch])
        if extra_waits:
            for wch, wc in extra_waits.items():
                waits[wch] = max(waits.get(wch, 0), wc)
        waits = {wch: wc for wch, wc in waits.items()
                 if self.obs[e].get(wch, 0) < wc}
        for wch, wc in sorted(waits.items()):
            self._observe(e, wch, wc)
        if inc:
            self.cnt[ch] += step
            snap = dict(self.obs[e])
            snap[e] = self.cnt[e]
            snap[ch] = self.cnt[ch]
            self.clock_at[(ch, self.cnt[ch])] = snap
            c_rec = self.cnt[ch]
        else:
            c_rec = self.cnt[ch] + step
        self.prog[e].append((waits, emit, inc, step, ch))
        for ap in writes:
            ext = _ap_extent(ap)
            lst = self.writes.setdefault(ext[0], [])
            lst[:] = [w for w in lst if not (w[1] == ch and w[2] <= c_rec
                                             and _covers(ext, w[0]))]
            lst.append((ext, ch, c_rec))
            rl = self.reads.get(ext[0])
            if rl:
                rl[:] = [r for r in rl if not _covers(ext, r[0])]
        for ap in reads:
            ext = _ap_extent(ap)
            self.reads.setdefault(ext[0], []).append((ext, ch, c_rec))

    def dma(self, e, out, in_, chain, **kw):
        self.op(e, lambda eng: eng.dma_start(out=out, in_=in_, **kw),
                reads=[in_], writes=[out], dma=True, dma_chain=chain)

    def barrier_value(self, ch):
        return self.cnt[ch]

    def wait(self, e, ch=None, c=None):
        if ch is None:
            return
        if self.obs[e].get(ch, 0) >= c:
            return
        self._observe(e, ch, c)
        self.prog[e].append(({ch: c}, None, False, 0, e))

    def finalize(self, block):
        name_map = {"t": "tensor", "v": "vector", "s": "scalar",
                    "g": "gpsimd", "q": "sync"}
        for e in ENGS:
            plist = self.prog[e]
            if not plist:
                continue
            def run_all(engine, plist=plist):
                for (waits, emit, inc, step, ch) in plist:
                    for wch, wc in sorted(waits.items()):
                        engine.wait_ge(self.chain[wch], wc)
                    if emit is None:
                        continue
                    inst = emit(engine)
                    if inc:
                        if inst is None:
                            inst = engine.nop()
                        inst.then_inc(self.chain[ch], step)
            getattr(block, name_map[e])(run_all)


D = 1024
NH, DK, DV = 4, 1024, 2048
HDK, HDV = 256, 512
LOWRANK = 16
GATE_NORM = 16.0
CHUNK = 64
EPS = 1e-5
E_EXP, CF = 32, 1.25
LN16 = math.log(16.0)


def quake_rsqrt(k, out, in_, tmp_i, tmp_b, e="v", iters=2):
    k.op(e, lambda v: v.tensor_scalar(
        tmp_i, in_.bitcast(I32), 1, None, ALU.logical_shift_right),
        reads=[in_], writes=[tmp_i])
    k.op(e, lambda v: v.tensor_scalar(
        tmp_i, tmp_i, -1.0, float(0x5F3759DF), ALU.mult, ALU.add),
        reads=[tmp_i], writes=[tmp_i])
    y = out
    k.op(e, lambda v: v.tensor_copy(y, tmp_i.bitcast(F32)),
         reads=[tmp_i], writes=[y])
    for _ in range(iters):
        k.op(e, lambda v: v.tensor_mul(tmp_b, y, y), reads=[y], writes=[tmp_b])
        k.op(e, lambda v: v.tensor_mul(tmp_b, tmp_b, in_),
             reads=[tmp_b, in_], writes=[tmp_b])
        k.op(e, lambda v: v.tensor_scalar(
            tmp_b, tmp_b, -0.5, 1.5, ALU.mult, ALU.add),
            reads=[tmp_b], writes=[tmp_b])
        k.op(e, lambda v: v.tensor_mul(y, y, tmp_b),
             reads=[y, tmp_b], writes=[y])


def kernel(**inputs):
    raise NotImplementedError


class Arena:

    def __init__(self, nc, name, cols_f32):
        self.t = nc.alloc_sbuf_tensor(name, [128, cols_f32], F32)
        self.cols = cols_f32

    def view(self, off, shape, dtype=F32):
        n = 1
        for s in shape[1:]:
            n *= s
        nf = (n * mybir.dt.size(dtype) + 3) // 4
        assert off + nf <= self.cols, (off, nf, self.cols)
        ap = self.t[:, off:off + nf]
        if dtype != F32:
            ap = ap.bitcast(dtype)
        ap = ap[:, :n]
        if len(shape) == 3:
            ap = ap.rearrange("p (a b) -> p a b", a=shape[1])
        if shape[0] < 128:
            ap = ap[0:shape[0]]
        return ap


def ceil_div(a, b):
    return -(-a // b)


def build_core(L=4096, n_cores=8, cap=None, probe=None, phase_limit=4, pa=4):
    NT = L // 512
    NSUB = 4
    NBLK = L // 128
    T_tot = n_cores * L
    CAP = cap if cap is not None else int(math.ceil(CF * T_tot / E_EXP))
    EPC = E_EXP // n_cores
    SLOTS = EPC * CAP
    assert SLOTS % 128 == 0 and CAP % 128 == 0
    NSB = SLOTS // 128
    BPE = CAP // 128
    HCAP = CAP // 2
    assert CAP % 2 == 0
    GRID = E_EXP * CAP
    INVN = GRID + 128
    INVROWS = ceil_div(INVN, 128)
    RG = [list(range(n_cores))]

    nc = bass.Bass("TRN2", target_bir_lowering=False, debug=False)

    din = {}
    def inp(name, shape, dt=F32):
        din[name] = nc.dram_tensor(name, shape, dt, kind="ExternalInput").ap()
        return din[name]
    x_in = inp("x", [L, D])
    wq_in = inp("wq", [D, DK]); wk_in = inp("wk", [D, DK])
    wv_in = inp("wv", [D, DV]); wg_in = inp("wg", [D, DV])
    wgk1_in = inp("wgk1", [D, LOWRANK]); wgk2_in = inp("wgk2", [LOWRANK, DK])
    bgk2_in = inp("bgk2", [1, DK]); gnw_in = inp("gnw", [1, HDV])
    wo_in = inp("wo", [DV, D]); wr_in = inp("wr", [D, E_EXP])
    wgate_in = inp("wgate", [EPC, D, D]); wup_in = inp("wup", [EPC, D, D])
    wdown_in = inp("wdown", [EPC, D, D])
    coremask_in = inp("coremask", [8, 1])
    tokid_in = inp("tokid", [128, NBLK])
    myslots_in = inp("myslots", [128, ceil_div(NSB, 16)], I16)

    y_out = nc.dram_tensor("y", [L, D], F32, kind="ExternalOutput").ap()

    od = nc.dram_tensor("od", [L, DV], BF16, kind="Internal").ap()
    x1d = nc.dram_tensor("x1d", [L, D], F32, kind="Internal").ap()
    cin_h = nc.dram_tensor("cin_h", [1, E_EXP], F32, kind="Internal").ap()
    cout_h = nc.dram_tensor("cout_h", [n_cores, E_EXP], F32, kind="Internal").ap()
    invd = nc.dram_tensor("invd", [INVROWS * 128], F32, kind="Internal").ap()
    invr = nc.dram_tensor("invr", [INVROWS * 128], F32, kind="Internal").ap()
    h2d = nc.dram_tensor("h2d", [L, D], BF16, kind="Internal").ap()
    h2ag = nc.dram_tensor("h2ag", [T_tot, D], BF16, kind="Internal").ap()
    yin = nc.dram_tensor("yin", [SLOTS, D], BF16, kind="Internal").ap()
    yag = nc.dram_tensor("yag", [GRID, D], BF16, kind="Internal").ap()

    with ExitStack() as st:
        k = K(nc, st)
        for ch in ("dx0", "dx1", "dw0", "dw1", "dos0", "dos1", "dor0", "dor1",
                   "dx1s", "dh2s", "dcc", "dci", "cc", "lib", "dsc",
                   "dsc0", "dsc1", "dsc2", "dsc3",
                   "dgh0", "dgh1", "dgh2", "dgh3",
                   "dys", "dyg0", "dyg1", "dyg2", "dyg3",
                   "dxf0", "dxf1", "dxf2", "dxf3", "dout0", "dout1",
                   "icc0", "icc1"):
            k.new_chain(ch)
        for ti_ in range(NT):
            k.new_chain("hcc%d" % ti_)
        for eh_ in range(2 * EPC):
            k.new_chain("ycc%d" % eh_)

        sbuf = lambda name, shape, dt=F32: nc.alloc_sbuf_tensor(
            name, shape, dt)

        S = sbuf("S", [128, 2 * NH, HDV])
        identb = sbuf("identb", [128, 128], BF16)
        identf = sbuf("identf", [128, 128], F32)
        onesb = sbuf("onesb", [128, 128], BF16)
        Lstrict = sbuf("Lstrict", [128, 128], BF16)
        L64p = sbuf("L64p", [128, 2, 64], F32)
        Lshp = sbuf("Lshp", [128, 2, 64], F32)
        maskT = sbuf("maskT", [128, 64], F32)
        onesf = sbuf("onesf", [128, 128], F32)
        gnwb = sbuf("gnwb", [128, DV], BF16)
        wgk2x = sbuf("wgk2x", [128, DK], BF16)
        wgk1b = sbuf("wgk1b", [128, 8, LOWRANK], BF16)
        wrf = sbuf("wrf", [128, 8, E_EXP], F32)
        colr = sbuf("colr", [1, E_EXP], F32)
        t1s = sbuf("t1s", [128, 512], BF16)
        gnw_st = sbuf("gnw_st", [1, HDV], F32)
        bgk2_st = sbuf("bgk2_st", [1, DK], F32)
        bgk2_hf = sbuf("bgk2_hf", [1, DK], F32)
        coremask_sb = sbuf("coremask_sb", [8, 1], F32)
        tokid_sb = sbuf("tokid_sb", [128, NBLK], F32)
        myslots_sb = sbuf("myslots_sb", [128, ceil_div(NSB, 16)], I16)
        histag = sbuf("histag", [n_cores, E_EXP], F32)
        goffb = sbuf("goffb", [128, E_EXP], F32)
        m1r1 = sbuf("m1r1", [128, NBLK * 2], F32)
        m2r2 = sbuf("m2r2", [128, NBLK * 2], F32)
        r2s = sbuf("r2s", [128, NBLK], F32)
        nm2s = sbuf("nm2s", [128, NBLK], F32)
        top1 = sbuf("top1", [128, NBLK], F32)
        gatem = sbuf("gatem", [128, NBLK], F32)
        poss = sbuf("poss", [128, NBLK], F32)
        onehot = sbuf("onehot", [128, NBLK * E_EXP], BF16)
        slotu = sbuf("slotu", [128, NBLK], U32)
        slotcl = sbuf("slotcl", [128, NBLK], U32)
        eblast = sbuf("eblast", [128, 64], F32)
        hist_sb = sbuf("hist_sb", [1, E_EXP], F32)
        invT = sbuf("invT", [128, NSB], U32)
        scr_i = sbuf("scr_i", [128, 64], I32)
        scr_a = sbuf("scr_a", [128, 64], F32)
        scr_b = sbuf("scr_b", [128, 64], F32)
        stats6 = sbuf("stats6", [128, 24], F32)
        aggr2 = sbuf("aggr2", [128, 8], F32)
        zerot = sbuf("zerot", [128, 128], F32)

        WA = Arena(nc, "WA", 16384)
        TB = Arena(nc, "TB", 21760)

        psum = nc.alloc_psum_tensor("psA", [128, 4096], F32)
        PB = lambda i, n=1: psum[:, i * 512:(i + n) * 512]

        k.op("g", lambda e: e.memset(onesb[:, :], 1.0), writes=[onesb[:, :]])
        k.op("g", lambda e: e.memset(onesf[:, :], 1.0), writes=[onesf[:, :]])
        k.op("g", lambda e: e.memset(zerot[:, :], 0.0), writes=[zerot[:, :]])
        k.op("g", lambda e: e.affine_select(
            identb[:, :], onesb[:, :], [[-1, 128]], ALU.is_equal, 0.0,
            base=0, channel_multiplier=1),
            reads=[onesb[:, :]], writes=[identb[:, :]])
        k.op("g", lambda e: e.affine_select(
            identf[:, :], onesf[:, :], [[-1, 128]], ALU.is_equal, 0.0,
            base=0, channel_multiplier=1),
            reads=[onesf[:, :]], writes=[identf[:, :]])
        k.op("g", lambda e: e.affine_select(
            Lstrict[:, :], onesb[:, :], [[1, 128]], ALU.is_gt, 0.0,
            base=0, channel_multiplier=-1),
            reads=[onesb[:, :]], writes=[Lstrict[:, :]])
        k.op("g", lambda e: e.memset(scr_a[:, 0:64], -1.0 / GATE_NORM),
             writes=[scr_a[:, 0:64]])
        k.op("g", lambda e: e.memset(L64p[:, :, :], 0.0),
             writes=[L64p[:, :, :]])
        k.op("g", lambda e: e.memset(Lshp[:, :, :], 0.0),
             writes=[Lshp[:, :, :]])
        for hh in range(2):
            sl = slice(hh * 64, (hh + 1) * 64)
            k.op("g", lambda e, sl=sl, hh=hh: e.affine_select(
                L64p[sl, hh, :], scr_a[sl, 0:64], [[1, 64]], ALU.is_ge, 0.0,
                base=0, channel_multiplier=-1),
                reads=[scr_a[:, 0:64]], writes=[L64p[sl, hh, :]])
            k.op("g", lambda e, sl=sl, hh=hh: e.affine_select(
                Lshp[sl, hh, :], scr_a[sl, 0:64], [[-1, 64]], ALU.is_gt, 0.0,
                base=0, channel_multiplier=1),
                reads=[scr_a[:, 0:64]], writes=[Lshp[sl, hh, :]])
            k.op("g", lambda e, sl=sl: e.affine_select(
                maskT[sl, :], onesf[sl, 0:64], [[1, 64]], ALU.is_ge, 0.0,
                base=0, channel_multiplier=-1),
                reads=[onesf[:, :]], writes=[maskT[sl, :]])
        k.dma("q", coremask_sb[:, :], coremask_in[:, :], "dcc")
        k.dma("q", tokid_sb[:, :], tokid_in[:, :], "dcc")
        k.dma("q", myslots_sb[:, :], myslots_in[:, :], "dcc")
        k.dma("q", gnw_st[:, :], gnw_in[:, :], "dcc")
        k.dma("q", bgk2_st[:, :], bgk2_in[:, :], "dcc")
        for h in range(NH):
            k.op("t", lambda e, h=h: e.matmul(
                PB(0)[:, 0:HDV], onesf[0:1, 0:128], gnw_st[0:1, :],
                start=True, stop=True),
                reads=[onesf[:, :], gnw_st[:, :]], writes=[PB(0)[:, 0:HDV]],
                sig=("k", 1, 0))
            k.op("v", lambda e, h=h: e.tensor_copy(
                gnwb[:, h * HDV:(h + 1) * HDV], PB(0)[:, 0:HDV]),
                reads=[PB(0)[:, 0:HDV]], writes=[gnwb[:, :]])
        k.dma("g", wgk2x[0:16, :], wgk2_in[:, :], "dw0")
        bgk2_hl = sbuf("bgk2_hl", [1, 2 * DK], BF16)
        k.op("v", lambda e: e.tensor_copy(bgk2_hl[:, 0:DK], bgk2_st[:, :]),
             reads=[bgk2_st[:, :]], writes=[bgk2_hl[:, 0:DK]])
        k.op("v", lambda e: e.tensor_copy(bgk2_hf[:, :], bgk2_hl[:, 0:DK]),
             reads=[bgk2_hl[:, 0:DK]], writes=[bgk2_hf[:, :]])
        k.op("v", lambda e: e.tensor_sub(bgk2_hl[:, DK:2 * DK], bgk2_st[:, :],
                                         bgk2_hf[:, :]),
             reads=[bgk2_st[:, :], bgk2_hf[:, :]],
             writes=[bgk2_hl[:, DK:2 * DK]])
        k.dma("g", wgk2x[16:17, :], bgk2_hl[:, 0:DK], "dw0")
        k.dma("g", wgk2x[17:18, :], bgk2_hl[:, DK:2 * DK], "dw0")
        k.op("g", lambda e: e.memset(t1s[0:32, :], 1.0),
             writes=[t1s[0:32, :]])
        for kk in range(8):
            k.dma("q", wrf[:, kk, :], wr_in[kk * 128:(kk + 1) * 128, :], "dcc")
        for kk in range(8):
            k.op("t", lambda e, kk=kk: e.matmul(
                PB(1)[0:1, 0:E_EXP], onesf[:, 0:1], wrf[:, kk, :],
                start=(kk == 0), stop=(kk == 7)),
                reads=[onesf[:, :], wrf[:, kk, :]],
                writes=[PB(1)[0:1, 0:E_EXP]], inc=(kk == 7),
                sig=("k", 128, 0))
        k.op("v", lambda e: e.tensor_copy(colr[:, :], PB(1)[0:1, 0:E_EXP]),
             reads=[PB(1)[0:1, 0:E_EXP]], writes=[colr[:, :]])
        colrb = sbuf("colrb", [128, E_EXP], F32)
        k.op("t", lambda e: e.matmul(
            PB(2)[:, 0:E_EXP], onesf[0:1, 0:128], colr[:, :],
            start=True, stop=True),
            reads=[onesf[:, :], colr[:, :]], writes=[PB(2)[:, 0:E_EXP]],
            sig=("k", 1, 0))
        k.op("v", lambda e: e.tensor_copy(colrb[:, :], PB(2)[:, 0:E_EXP]),
             reads=[PB(2)[:, 0:E_EXP]], writes=[colrb[:, :]])

        wqs = WA.view(0, [128, 8, DK], BF16)
        wks = WA.view(4096, [128, 8, DK], BF16)
        wvs = WA.view(8192, [128, 8, DV], BF16)
        for kk in range(8):
            src = [(wqs, wq_in), (wks, wk_in)][0]
            k.dma("g", wqs[:, kk, :], wq_in[kk * 128:(kk + 1) * 128, :], "dw0")
            k.dma("g", wks[:, kk, :], wk_in[kk * 128:(kk + 1) * 128, :], "dw1")
            k.dma("g", wvs[:, kk, :], wv_in[kk * 128:(kk + 1) * 128, :], "dw0")
            k.dma("g", wgk1b[:, kk, :], wgk1_in[kk * 128:(kk + 1) * 128, :], "dw1")

        Sb = sbuf("Sb", [128, 2 * NH, HDV], BF16)
        scTe = sbuf("scTe", [128, NH, 64], BF16)
        scTo = sbuf("scTo", [128, NH, 64], BF16)
        oStage = TB.view(3072, [64, 2, DV], BF16)

        k.op("v", lambda e: e.memset(S[:, :, :], 0.0), writes=[S[:, :, :]])
        k.op("v", lambda e: e.memset(scTe[:, :, :], 0.0), writes=[scTe[:, :, :]])
        k.op("v", lambda e: e.memset(scTo[:, :, :], 0.0), writes=[scTo[:, :, :]])
        ln16c = sbuf("ln16c", [128, 1], F32)
        k.op("v", lambda e: e.memset(ln16c[:, :], -LN16), writes=[ln16c[:, :]])
        k.op("g", lambda e: e.memset(Sb[:, :, :], 0.0), writes=[Sb[:, :, :]])

        xsub = [TB.view(0, [128, D]), TB.view(1024, [128, D])]
        h1sub = [TB.view(2048, [128, D], BF16), TB.view(2560, [128, D], BF16)]
        h1fm = TB.view(3072, [128, 8, 512], BF16)
        qfm = TB.view(5120, [128, 8, 512], BF16)
        kfm = TB.view(7168, [128, 8, 512], BF16)
        ktm = TB.view(9216, [128, 4, D], BF16)
        vtm = TB.view(11264, [128, 4, DV], BF16)
        sp = TB.view(15360, [128, 4, D], F32)
        ebuf = TB.view(19456, [128, 512], BF16)
        ebsh_t = TB.view(19712, [128, D], BF16)
        kst2 = TB.view(20224, [128, 2, D], BF16)
        k.op("v", lambda e: e.memset(kst2[:, :, :], 0.0), writes=[kst2[:, :, :]])

        ptp = [psum[:, 2048:2112].bitcast(BF16),
               psum[:, 2560:2624].bitcast(BF16),
               psum[:, 2112:2176].bitcast(BF16),
               psum[:, 2624:2688].bitcast(BF16)]

        def ln_stats(xs, negmr_ap, r_ap, blkw=1024):
            nh = blkw // 512
            for i in range(nh):
                k.op("v", lambda e, i=i: e.bn_stats(
                    stats6[:, 6 * i:6 * i + 6], xs[:, 512 * i:512 * (i + 1)]),
                    reads=[xs], writes=[stats6[:, 6 * i:6 * i + 6]])
            k.op("v", lambda e: e.bn_aggr(aggr2[:, 0:2], stats6[:, 0:6 * nh]),
                 reads=[stats6[:, 0:6 * nh]], writes=[aggr2[:, 0:2]])
            k.op("v", lambda e: e.tensor_scalar_add(
                scr_a[:, 0:1], aggr2[:, 1:2], EPS),
                reads=[aggr2[:, 0:2]], writes=[scr_a[:, 0:1]])
            quake_rsqrt(k, r_ap, scr_a[:, 0:1], scr_i[:, 0:1], scr_b[:, 0:1])
            k.op("v", lambda e: e.tensor_scalar(
                negmr_ap, aggr2[:, 0:1], r_ap, -1.0, ALU.mult, ALU.mult),
                reads=[aggr2[:, 0:2], r_ap], writes=[negmr_ap])

        for ti in range(NT):
            trow = ti * 512
            for sub in range(NSUB):
                blk = ti * NSUB + sub
                xs = xsub[sub % 2]
                k.dma("q", xs, x_in[trow + sub * 128: trow + (sub + 1) * 128, :],
                      "dx%d" % (sub % 2))
                nmr = m1r1[:, 2 * blk:2 * blk + 1]
                rr = m1r1[:, 2 * blk + 1:2 * blk + 2]
                ln_stats(xs, nmr, rr)
                hs = h1sub[sub % 2]
                k.op("s", lambda e, xs=xs, hs=hs, nmr=nmr, rr=rr: e.activation(
                    hs, xs, AF.Identity, bias=nmr, scale=rr),
                    reads=[xs, nmr, rr], writes=[hs])
                for f in range(8):
                    pt = ptp[f % 4]
                    k.op("t", lambda e, hs=hs, f=f, pt=pt: e.transpose(
                        pt, hs[:, f * 128:(f + 1) * 128], identb[:, :]),
                        reads=[hs, identb[:, :]], writes=[pt],
                        sig=("k", 128, 0))
                    dst = h1fm[:, f, sub * 128:(sub + 1) * 128]
                    eng = "s"
                    k.op(eng, lambda e, dst=dst, pt=pt: (
                        e.tensor_copy(dst, pt) if hasattr(e, "tensor_copy")
                        else e.copy(dst, pt)),
                        reads=[pt], writes=[dst])

            if pa >= 2:
                for m in range(8):
                    pq = PB(m % 4)
                    for kk in range(8):
                        k.op("t", lambda e, m=m, kk=kk, pq=pq: e.matmul(
                            pq, wqs[:, kk, m * 128:(m + 1) * 128], h1fm[:, kk, :],
                            start=(kk == 0), stop=(kk == 7)),
                            reads=[wqs[:, kk, :], h1fm[:, kk, :]], writes=[pq],
                            inc=(kk == 7), sig=("k", 128, 0))
                    eng = "v" if m % 2 else "s"
                    k.op(eng, lambda e, m=m, pq=pq: (
                        e.tensor_copy(qfm[:, m, :], pq) if hasattr(e, "tensor_copy")
                        else e.copy(qfm[:, m, :], pq)),
                        reads=[pq], writes=[qfm[:, m, :]])
                for m in range(8):
                    pq = PB(m % 4)
                    for kk in range(8):
                        k.op("t", lambda e, m=m, kk=kk, pq=pq: e.matmul(
                            pq, wks[:, kk, m * 128:(m + 1) * 128], h1fm[:, kk, :],
                            start=(kk == 0), stop=(kk == 7)),
                            reads=[wks[:, kk, :], h1fm[:, kk, :]], writes=[pq],
                            inc=(kk == 7), sig=("k", 128, 0))
                    eng = "v" if m % 2 else "s"
                    k.op(eng, lambda e, m=m, pq=pq: (
                        e.tensor_copy(kfm[:, m, :], pq) if hasattr(e, "tensor_copy")
                        else e.copy(kfm[:, m, :], pq)),
                        reads=[pq], writes=[kfm[:, m, :]])
                for sub in range(NSUB):
                    pq = PB(4 if sub % 2 == 0 else 6, 2)
                    for nh in range(2):
                        for kk in range(8):
                            k.op("t", lambda e, sub=sub, kk=kk, nh=nh, pq=pq:
                                 e.matmul(
                                     pq[:, nh * 512:(nh + 1) * 512],
                                     h1fm[:, kk, sub * 128:(sub + 1) * 128],
                                     wks[:, kk, nh * 512:(nh + 1) * 512],
                                     start=(kk == 0), stop=(kk == 7)),
                                 reads=[wks[:, kk, :], h1fm[:, kk, :]],
                                 writes=[pq[:, nh * 512:(nh + 1) * 512]],
                                 inc=(kk == 7), sig=("k", 128, 0))
                    eng = "v" if sub % 2 else "s"
                    k.op(eng, lambda e, sub=sub, pq=pq: (
                        e.tensor_copy(ktm[:, sub, :], pq) if hasattr(e, "tensor_copy")
                        else e.copy(ktm[:, sub, :], pq)),
                        reads=[pq], writes=[ktm[:, sub, :]])
                for sub in range(NSUB):
                    for half in range(2):
                        pq = PB(4 if (2 * sub + half) % 2 == 0 else 6, 2)
                        for nh in range(2):
                            for kk in range(8):
                                k.op("t", lambda e, sub=sub, kk=kk, half=half,
                                     nh=nh, pq=pq: e.matmul(
                                         pq[:, nh * 512:(nh + 1) * 512],
                                         h1fm[:, kk, sub * 128:(sub + 1) * 128],
                                         wvs[:, kk,
                                             half * 1024 + nh * 512:
                                             half * 1024 + (nh + 1) * 512],
                                         start=(kk == 0), stop=(kk == 7)),
                                     reads=[wvs[:, kk, :], h1fm[:, kk, :]],
                                     writes=[pq[:, nh * 512:(nh + 1) * 512]],
                                     inc=(kk == 7), sig=("k", 128, 0))
                        dst = vtm[:, sub, half * 1024:(half + 1) * 1024]
                        eng = "v" if half else "s"
                        k.op(eng, lambda e, dst=dst, pq=pq: (
                            e.tensor_copy(dst, pq) if hasattr(e, "tensor_copy")
                            else e.copy(dst, pq)),
                            reads=[pq], writes=[dst])
                pt1 = PB(0)
                for kk in range(8):
                    k.op("t", lambda e, kk=kk: e.matmul(
                        pt1[0:16, :], wgk1b[:, kk, :], h1fm[:, kk, :],
                        start=(kk == 0), stop=(kk == 7)),
                        reads=[wgk1b[:, kk, :], h1fm[:, kk, :]],
                        writes=[pt1[0:16, :]], inc=(kk == 7), sig=("k", 128, 0))
                k.op("v", lambda e: e.tensor_copy(t1s[0:16, :], pt1[0:16, :]),
                     reads=[pt1[0:16, :]], writes=[t1s[0:16, :]])
                for sub in range(NSUB):
                    pq = PB(1 if sub % 2 == 0 else 2, 2)
                    pq = PB(0, 2) if sub % 2 == 0 else PB(2, 2)
                    for nh in range(2):
                        k.op("t", lambda e, sub=sub, nh=nh, pq=pq: e.matmul(
                            pq[:, nh * 512:(nh + 1) * 512],
                            t1s[0:18, sub * 128:(sub + 1) * 128],
                            wgk2x[0:18, nh * 512:(nh + 1) * 512],
                            start=True, stop=True),
                            reads=[t1s[0:18, :], wgk2x[0:18, :]],
                            writes=[pq[:, nh * 512:(nh + 1) * 512]],
                            inc=(nh == 1), sig=("k", 18, 0))
                    spv = sp[:, sub, :]
                    k.op("s", lambda e, spv=spv, pq=pq: e.activation(
                        spv, pq, AF.Exp, scale=-1.0),
                        reads=[pq], writes=[spv])
                    k.op("v", lambda e, spv=spv: e.tensor_scalar_add(spv, spv, 1.0),
                         reads=[spv], writes=[spv])
                    k.op("s", lambda e, spv=spv: e.activation(spv, spv, AF.Ln),
                         reads=[spv], writes=[spv])

            if pa >= 3:
                for ht in range(8):
                    pq = PB(4) if ht % 2 == 0 else PB(5)
                    for c in range(8):
                        k.op("t", lambda e, ht=ht, c=c, pq=pq: e.matmul(
                            pq[:, c * 64:(c + 1) * 64],
                            sp[:, c // 2, ht * 128:(ht + 1) * 128],
                            L64p[:, c % 2, :],
                            start=True, stop=True),
                            reads=[sp[:, c // 2, :], L64p[:, :, :]],
                            writes=[pq[:, c * 64:(c + 1) * 64]],
                            inc=(c == 7), sig=("k", 128, 0))
                    pr = pq.rearrange("p (c e) -> p c e", c=8)
                    k.op("s", lambda e, ht=ht, pr=pr: e.activation(
                        eblast[:, 8 * ht:8 * ht + 8], pr[:, :, 63:64], AF.Exp),
                        reads=[pr[:, :, 63:64]], writes=[eblast[:, 8 * ht:8 * ht + 8]])
                    k.op("s", lambda e, pq=pq: e.activation(
                        ebuf, pq, AF.Exp, bias=ln16c[:, :]),
                        reads=[pq, ln16c[:, :]], writes=[ebuf])
                    k.op("v", lambda e, ht=ht: e.tensor_mul(
                        qfm[:, ht, :], qfm[:, ht, :], ebuf),
                        reads=[qfm[:, ht, :], ebuf], writes=[qfm[:, ht, :]])
                    k.op("s", lambda e, pq=pq: e.activation(
                        ebuf, pq, AF.Exp, scale=-1.0),
                        reads=[pq], writes=[ebuf])
                    k.op("v", lambda e, ht=ht: e.tensor_mul(
                        kfm[:, ht, :], kfm[:, ht, :], ebuf),
                        reads=[kfm[:, ht, :], ebuf], writes=[kfm[:, ht, :]])

            if pa >= 4:
                ov_all = psum[0:64, 1536:3584]
                for c in range(8):
                    sub, r0 = c // 2, (c % 2) * 64
                    par = c % 2
                    cs = slice(c * 64, (c + 1) * 64)
                    scp = (scTe, scTo)[par]
                    if par == 0:
                        for nh in range(2):
                            pb7 = PB(7)
                            for p2 in range(2):
                                k.op("t", lambda e, sub=sub, nh=nh, p2=p2, pb7=pb7: e.matmul(
                                    pb7[p2 * 64:(p2 + 1) * 64, :],
                                    Lshp[:, p2, :],
                                    sp[:, sub, nh * 512:(nh + 1) * 512],
                                    start=True, stop=True, tile_position=(0, p2 * 64)),
                                    reads=[Lshp[:, :, :], sp[:, sub, :]],
                                    writes=[pb7[p2 * 64:(p2 + 1) * 64, :]],
                                    inc=(p2 == 1), sig=("k", 128, 0))
                            k.op("s", lambda e, nh=nh, pb7=pb7: e.activation(
                                ebsh_t[:, nh * 512:(nh + 1) * 512], pb7, AF.Exp),
                                reads=[pb7], writes=[ebsh_t[:, nh * 512:(nh + 1) * 512]])
                        for p2 in range(2):
                            q0 = p2 * 64
                            k.op("v", lambda e, sub=sub, p2=p2, q0=q0: e.tensor_mul(
                                kst2[q0:q0 + 64, p2, :], ktm[q0:q0 + 64, sub, :],
                                ebsh_t[q0:q0 + 64, :]),
                                reads=[ktm[:, sub, :], ebsh_t[:, :]],
                                writes=[kst2[q0:q0 + 64, p2, :]])
                    for h in range(NH):
                        sc = PB(2)[r0:r0 + 64, h * 64:(h + 1) * 64]
                        for j in range(2):
                            ht = 2 * h + j
                            k.op("t", lambda e, ht=ht, cs=cs, sc=sc, r0=r0, j=j:
                                 e.matmul(
                                     sc, kfm[:, ht, cs], qfm[:, ht, cs],
                                     start=(j == 0), stop=(j == 1),
                                     tile_position=(0, r0)),
                                 reads=[kfm[:, ht, cs], qfm[:, ht, cs]],
                                 writes=[sc], inc=(j == 1),
                                 sig=("k", 128, 0))
                    for h in range(NH):
                        sc = PB(2)[r0:r0 + 64, h * 64:(h + 1) * 64]
                        k.op("v", lambda e, h=h, sc=sc, r0=r0, scp=scp: e.tensor_mul(
                            scp[r0:r0 + 64, h, :], sc, maskT[r0:r0 + 64, :]),
                            reads=[sc, maskT[:, :]], writes=[scp[r0:r0 + 64, h, :]])
                    for h in range(NH):
                        ov = ov_all[:, h * 512:(h + 1) * 512]
                        vsl = vtm[:, sub, h * 512:(h + 1) * 512]
                        k.op("t", lambda e, h=h, ov=ov, vsl=vsl, scp=scp: e.matmul(
                            ov, scp[:, h, :], vsl, start=True, stop=False),
                            reads=[scp[:, h, :], vsl], writes=[ov],
                            inc=False, sig=("k", 128, 0))
                    for h in range(NH):
                        ov = ov_all[:, h * 512:(h + 1) * 512]
                        for j in range(2):
                            ht = 2 * h + j
                            k.op("t", lambda e, ht=ht, cs=cs, ov=ov, j=j: e.matmul(
                                ov, qfm[:, ht, cs], Sb[:, ht, :],
                                start=False, stop=(j == 1)),
                                reads=[qfm[:, ht, cs], Sb[:, ht, :]],
                                writes=[ov], inc=(j == 1),
                                sig=("k", 128, 0))
                    for h in range(NH):
                        for j in range(2):
                            ht = 2 * h + j
                            pd = PB(j)
                            ksl = kst2[:, par, h * 256 + j * 128: h * 256 + (j + 1) * 128]
                            vsl2 = vtm[:, sub, h * 512:(h + 1) * 512]
                            k.op("t", lambda e, pd=pd, ksl=ksl, vsl2=vsl2:
                                 e.matmul(pd, ksl, vsl2, start=True, stop=True),
                                 reads=[ksl, vsl2], writes=[pd],
                                 sig=("k", 128, 0))
                            k.op("v", lambda e, ht=ht, c=c, pd=pd:
                                 e.scalar_tensor_tensor(
                                     S[:, ht, :], S[:, ht, :],
                                     eblast[:, 8 * ht + c:8 * ht + c + 1], pd,
                                     ALU.mult, ALU.add),
                                 reads=[S[:, ht, :],
                                        eblast[:, 8 * ht + c:8 * ht + c + 1], pd],
                                 writes=[S[:, ht, :]])
                            k.op("s", lambda e, ht=ht: e.copy(
                                Sb[:, ht, :], S[:, ht, :]),
                                reads=[S[:, ht, :]], writes=[Sb[:, ht, :]])
                    ost = oStage[0:64, c % 2, :]
                    eng = "s" if c % 2 == 0 else "v"
                    k.op(eng, lambda e, ost=ost: (
                        e.copy(ost, ov_all) if hasattr(e, "copy")
                        else e.tensor_copy(ost, ov_all)),
                        reads=[ov_all], writes=[ost])
                    k.dma("g", od[trow + c * 64: trow + (c + 1) * 64, :], ost,
                          "dos%d" % (c % 2))

        if phase_limit >= 2:
            wgB = WA.view(0, [128, 8, DV], BF16)
            woB = WA.view(8192, [128, 16, D], BF16)
            for kk in range(8):
                k.dma("g", wgB[:, kk, :], wg_in[kk * 128:(kk + 1) * 128, :], "dw0")
            for kk in range(16):
                k.dma("g", woB[:, kk, :], wo_in[kk * 128:(kk + 1) * 128, :], "dw1")

            h1fmB = TB.view(3072, [128, 8, 512], BF16)
            opair = [TB.view(5120, [128, DV], BF16), TB.view(6144, [128, DV], BF16)]
            gsil = TB.view(7168, [128, DV], BF16)
            ofsub = [TB.view(8192, [128, 16, 128], BF16),
                     TB.view(9216, [128, 16, 128], BF16)]
            x1sub = [TB.view(10240, [128, D]), TB.view(11264, [128, D])]
            x1fm = [TB.view(12288, [128, 8, 128]), TB.view(13312, [128, 8, 128])]
            h2sub = [TB.view(14336, [128, D], BF16), TB.view(14848, [128, D], BF16)]
            ofin = TB.view(15872, [128, DV], BF16)
            tbuf = TB.view(16384, [128, D])
            iotaE = sbuf("iotaE", [128, E_EXP], F32)
            zsb = sbuf("zsb", [128, 128], F32)
            nmrow = sbuf("nmrow", [1, 128], F32)
            ztm = sbuf("ztm", [128, E_EXP], F32)
            mx8 = sbuf("mx8", [128, 8], F32)
            ngmx = sbuf("ngmx", [128, 1], F32)
            zt2 = sbuf("zt2", [128, E_EXP], F32)
            thb = sbuf("thb", [128, E_EXP], F32)
            numb = sbuf("numb", [128, E_EXP], F32)
            denb = sbuf("denb", [128, E_EXP], F32)
            fracb = sbuf("fracb", [128, E_EXP], F32)
            selb = sbuf("selb", [128, E_EXP], F32)
            posf = sbuf("posf", [128, E_EXP], F32)
            ssum1 = sbuf("ssum1", [128, 1], F32)
            rms4 = sbuf("rms4", [128, 8], F32)

            k.op("g", lambda e: e.iota(scr_i[:, 0:E_EXP], [[1, E_EXP]],
                                       base=0, channel_multiplier=0),
                 writes=[scr_i[:, 0:E_EXP]])
            k.op("v", lambda e: e.tensor_copy(iotaE[:, :], scr_i[:, 0:E_EXP]),
                 reads=[scr_i[:, 0:E_EXP]], writes=[iotaE[:, :]])

            ptf = [psum[:, 2048:2176], psum[:, 2560:2688],
                   psum[:, 2176:2304], psum[:, 2688:2816]]
            pbase = PB(7)[:, 0:E_EXP]
            basef = sbuf("basef", [128, E_EXP], F32)
            k.op("v", lambda e: e.memset(basef[:, :], 0.0), writes=[basef[:, :]])

            NS = NT * NSUB
            xsubB = [TB.view(0, [128, D]), TB.view(1024, [128, D]),
                     TB.view(17408, [128, D]), TB.view(18432, [128, D])]

            def emitP(s):
                ti, sub = divmod(s, NSUB)
                trow = ti * 512
                xs = xsubB[s % 4]
                k.dma("q", xs, x_in[trow + sub * 128: trow + (sub + 1) * 128, :],
                      "dx%d" % (s % 2))
                ot = opair[s % 2]
                k.dma("q", ot, od[trow + sub * 128: trow + (sub + 1) * 128, :],
                      "dor%d" % (s % 2))

            def emitH1(s):
                ti, sub = divmod(s, NSUB)
                blk = s
                pp = s % 2
                xs = xsubB[s % 4]
                hs = h1sub[pp]
                nmr = m1r1[:, 2 * blk:2 * blk + 1]
                rr = m1r1[:, 2 * blk + 1:2 * blk + 2]
                k.op("s", lambda e, xs=xs, hs=hs, nmr=nmr, rr=rr: e.activation(
                    hs, xs, AF.Identity, bias=nmr, scale=rr),
                    reads=[xs, nmr, rr], writes=[hs])
                for f in range(8):
                    pt = ptp[f % 4]
                    k.op("t", lambda e, hs=hs, f=f, pt=pt: e.transpose(
                        pt, hs[:, f * 128:(f + 1) * 128], identb[:, :]),
                        reads=[hs, identb[:, :]], writes=[pt], sig=("k", 128, 0))
                    dst = h1fmB[:, f, sub * 128:(sub + 1) * 128]
                    eng = "s"
                    k.op(eng, lambda e, dst=dst, pt=pt: (
                        e.tensor_copy(dst, pt) if hasattr(e, "tensor_copy")
                        else e.copy(dst, pt)),
                        reads=[pt], writes=[dst])

            def emitS1a(s):
                ti, sub = divmod(s, NSUB)
                blk = s
                pp = s % 2
                pg = psum[:, 0:2048]
                for half in range(4):
                    for kk in range(8):
                        k.op("t", lambda e, sub=sub, kk=kk, half=half, pg=pg:
                             e.matmul(
                                 pg[:, half * 512:(half + 1) * 512],
                                 h1fmB[:, kk, sub * 128:(sub + 1) * 128],
                                 wgB[:, kk, half * 512:(half + 1) * 512],
                                 start=(kk == 0), stop=(kk == 7)),
                             reads=[h1fmB[:, kk, :], wgB[:, kk, :]],
                             writes=[pg[:, half * 512:(half + 1) * 512]],
                             inc=(kk == 7), sig=("k", 128, 0))
                k.op("s", lambda e, pg=pg: e.activation(gsil, pg, AF.Silu),
                     reads=[pg], writes=[gsil])
                k.op("v", lambda e: e.tensor_mul(gsil, gsil, gnwb[:, :]),
                     reads=[gsil, gnwb[:, :]], writes=[gsil])

            def emitRMS(s):
                pp = s % 2
                ot = opair[pp]
                r4 = rms4[:, 4 * pp:4 * pp + 4]
                for h in range(NH):
                    k.op("v", lambda e, h=h, ot=ot: e.bn_stats(
                        stats6[:, 6 * h:6 * h + 6],
                        ot[:, h * HDV:(h + 1) * HDV]),
                        reads=[ot], writes=[stats6[:, 6 * h:6 * h + 6]])
                    k.op("v", lambda e, h=h: e.bn_aggr(
                        aggr2[:, 2 * h:2 * h + 2], stats6[:, 6 * h:6 * h + 6]),
                        reads=[stats6[:, 6 * h:6 * h + 6]],
                        writes=[aggr2[:, 2 * h:2 * h + 2]])
                    k.op("v", lambda e, h=h: e.tensor_scalar(
                        scr_a[:, h:h + 1], aggr2[:, 2 * h:2 * h + 1],
                        aggr2[:, 2 * h:2 * h + 1], None, ALU.mult),
                        reads=[aggr2[:, 2 * h:2 * h + 2]],
                        writes=[scr_a[:, h:h + 1]])
                    k.op("v", lambda e, h=h: e.tensor_add(
                        scr_a[:, h:h + 1], scr_a[:, h:h + 1],
                        aggr2[:, 2 * h + 1:2 * h + 2]),
                        reads=[scr_a[:, h:h + 1], aggr2[:, 2 * h:2 * h + 2]],
                        writes=[scr_a[:, h:h + 1]])
                k.op("v", lambda e: e.tensor_scalar_add(
                    scr_a[:, 0:4], scr_a[:, 0:4], EPS),
                    reads=[scr_a[:, 0:4]], writes=[scr_a[:, 0:4]])
                quake_rsqrt(k, r4, scr_a[:, 0:4], scr_i[:, 0:4],
                            scr_b[:, 0:4])

            def emitS1b(s):
                ti, sub = divmod(s, NSUB)
                blk = s
                pp = s % 2
                xs = xsubB[s % 4]
                trow = ti * 512
                ot = opair[pp]
                r4 = rms4[:, 4 * pp:4 * pp + 4]
                k.op("v", lambda e, ot=ot: e.tensor_mul(ofin, ot, gsil),
                     reads=[ot, gsil], writes=[ofin])
                for h in range(NH):
                    eng = "s" if h % 2 else "v"
                    sl = ofin[:, h * HDV:(h + 1) * HDV]
                    if eng == "v":
                        k.op("v", lambda e, sl=sl, h=h, r4=r4: e.tensor_scalar_mul(
                            sl, sl, r4[:, h:h + 1]),
                            reads=[sl, r4], writes=[sl])
                    else:
                        k.op("s", lambda e, sl=sl, h=h, r4=r4: e.activation(
                            sl, sl, AF.Identity, scale=r4[:, h:h + 1]),
                            reads=[sl, r4], writes=[sl])
                for f in range(16):
                    pt = ptp[f % 4]
                    k.op("t", lambda e, f=f, pt=pt: e.transpose(
                        pt, ofin[:, f * 128:(f + 1) * 128], identb[:, :]),
                        reads=[ofin, identb[:, :]], writes=[pt], sig=("k", 128, 0))
                    dst = ofsub[pp][:, f, :]
                    eng = "v" if f % 2 else "s"
                    k.op(eng, lambda e, dst=dst, pt=pt: (
                        e.tensor_copy(dst, pt) if hasattr(e, "tensor_copy")
                        else e.copy(dst, pt)),
                        reads=[pt], writes=[dst])
                pw = psum[:, 0:1024]
                for nh in range(2):
                    for kk in range(16):
                        k.op("t", lambda e, kk=kk, pp=pp, nh=nh, pw=pw:
                             e.matmul(
                                 pw[:, nh * 512:(nh + 1) * 512],
                                 ofsub[pp][:, kk, :],
                                 woB[:, kk, nh * 512:(nh + 1) * 512],
                                 start=(kk == 0), stop=(kk == 15)),
                             reads=[ofsub[pp][:, kk, :], woB[:, kk, :]],
                             writes=[pw[:, nh * 512:(nh + 1) * 512]],
                             inc=(kk == 15), sig=("k", 128, 0))
                for i in range(2):
                    k.op("v", lambda e, i=i, pw=pw: e.bn_stats(
                        stats6[:, 6 * i:6 * i + 6],
                        pw[:, 512 * i:512 * (i + 1)]),
                        reads=[pw], writes=[stats6[:, 6 * i:6 * i + 6]])
                k.op("v", lambda e: e.bn_aggr(aggr2[:, 0:2], stats6[:, 0:12]),
                     reads=[stats6[:, 0:12]], writes=[aggr2[:, 0:2]])
                k.op("v", lambda e: e.tensor_scalar_add(
                    scr_a[:, 0:1], aggr2[:, 1:2], EPS),
                    reads=[aggr2[:, 0:2]], writes=[scr_a[:, 0:1]])
                quake_rsqrt(k, scr_b[:, 1:2], scr_a[:, 0:1], scr_i[:, 0:1],
                            scr_b[:, 0:1])
                k.op("v", lambda e: e.tensor_scalar(
                    scr_a[:, 1:2], aggr2[:, 0:1], scr_b[:, 1:2], -1.0,
                    ALU.mult, ALU.mult),
                    reads=[aggr2[:, 0:2], scr_b[:, 1:2]],
                    writes=[scr_a[:, 1:2]])
                k.op("s", lambda e, pw=pw: e.activation(
                    tbuf, pw, AF.Identity, bias=scr_a[:, 1:2],
                    scale=scr_b[:, 1:2]),
                    reads=[pw, scr_a[:, 1:2], scr_b[:, 1:2]], writes=[tbuf])
                x1s = x1sub[pp]
                k.op("v", lambda e, x1s=x1s, xs=xs: e.tensor_add(
                    x1s, xs, tbuf),
                    reads=[xs, tbuf], writes=[x1s])
                k.dma("s", x1d[trow + sub * 128: trow + (sub + 1) * 128, :],
                      x1s, "dx1s")

            def emitS1c(s):
                ti, sub = divmod(s, NSUB)
                blk = s
                pp = s % 2
                trow = ti * 512
                x1s = x1sub[pp]
                ln_stats(x1s, m2r2[:, 2 * blk:2 * blk + 1], m2r2[:, 2 * blk + 1:2 * blk + 2])
                k.op("v", lambda e, blk=blk: e.tensor_scalar_mul(
                    nm2s[:, blk:blk + 1], aggr2[:, 0:1], -1.0),
                    reads=[aggr2[:, 0:2]], writes=[nm2s[:, blk:blk + 1]])
                h2s = h2sub[pp]
                k.op("s", lambda e, x1s=x1s, h2s=h2s, blk=blk: e.activation(
                    h2s, x1s, AF.Identity, bias=m2r2[:, 2 * blk:2 * blk + 1],
                    scale=m2r2[:, 2 * blk + 1:2 * blk + 2]),
                    reads=[x1s, m2r2[:, 2 * blk:2 * blk + 1], m2r2[:, 2 * blk + 1:2 * blk + 2]],
                    writes=[h2s])
                k.dma("s", h2d[trow + sub * 128: trow + (sub + 1) * 128, :],
                      h2s, "dh2s")
                if sub == NSUB - 1 and ti < NT - 1:
                    emit_h2ag(ti)

            def emitS2(s):
                blk = s
                pp = s % 2
                x1s = x1sub[pp]
                for f in range(8):
                    pt = ptf[f % 4]
                    k.op("t", lambda e, x1s=x1s, f=f, pt=pt: e.transpose(
                        pt, x1s[:, f * 128:(f + 1) * 128], identf[:, :]),
                        reads=[x1s, identf[:, :]], writes=[pt], sig=("k", 128, 0))
                    dst = x1fm[pp][:, f, :]
                    eng = "s"
                    k.op(eng, lambda e, dst=dst, pt=pt: (
                        e.tensor_copy(dst, pt) if hasattr(e, "tensor_copy")
                        else e.copy(dst, pt)),
                        reads=[pt], writes=[dst])
                pz = PB(6)[0:32, 0:128]
                for kk in range(8):
                    k.op("t", lambda e, kk=kk, pp=pp, pz=pz: e.matmul(
                        pz, wrf[:, kk, :], x1fm[pp][:, kk, :],
                        start=(kk == 0), stop=(kk == 7)),
                        reads=[wrf[:, kk, :], x1fm[pp][:, kk, :]],
                        writes=[pz], inc=(kk == 7), sig=("k", 128, 0))
                k.op("s", lambda e, pz=pz: e.copy(zsb[0:32, :], pz),
                     reads=[pz], writes=[zsb[:, :]])
                k.op("t", lambda e: e.transpose(
                    ptf[1], zsb[:, :], identf[:, :]),
                    reads=[zsb[:, :], identf[:, :]], writes=[ptf[1]], sig=("k", 128, 0))
                k.op("s", lambda e: e.copy(ztm[:, :], ptf[1][:, 0:32]),
                     reads=[ptf[1]], writes=[ztm[:, :]])
                k.op("v", lambda e, blk=blk: e.scalar_tensor_tensor(
                    ztm[:, :], colrb[:, :], nm2s[:, blk:blk + 1], ztm[:, :],
                    ALU.mult, ALU.add),
                    reads=[colrb[:, :], nm2s[:, blk:blk + 1], ztm[:, :]],
                    writes=[ztm[:, :]])
                k.op("v", lambda e: e.max(mx8[:, :], ztm[:, :]),
                     reads=[ztm[:, :]], writes=[mx8[:, :]])
                k.op("v", lambda e, blk=blk: e.tensor_scalar(
                    onehot[:, blk * E_EXP:(blk + 1) * E_EXP], ztm[:, :], mx8[:, 0:1], None,
                    ALU.is_equal),
                    reads=[ztm[:, :], mx8[:, :]], writes=[onehot[:, blk * E_EXP:(blk + 1) * E_EXP]])
                k.op("v", lambda e, blk=blk: e.tensor_mul(
                    selb[:, :], onehot[:, blk * E_EXP:(blk + 1) * E_EXP], iotaE[:, :]),
                    reads=[onehot[:, blk * E_EXP:(blk + 1) * E_EXP], iotaE[:, :]], writes=[selb[:, :]])
                k.op("v", lambda e, blk=blk: e.reduce_sum(
                    top1[:, blk:blk + 1], selb[:, :], AX.X),
                    reads=[selb[:, :]], writes=[top1[:, blk:blk + 1]])
                k.op("v", lambda e: e.tensor_scalar_mul(ngmx[:, :], mx8[:, 0:1],
                                                        -1.0),
                     reads=[mx8[:, :]], writes=[ngmx[:, :]])
                k.op("v", lambda e, blk=blk: e.tensor_scalar(
                    zt2[:, :], ztm[:, :], ngmx[:, 0:1], m2r2[:, 2 * blk + 1:2 * blk + 2],
                    ALU.add, ALU.mult),
                    reads=[ztm[:, :], ngmx[:, :], m2r2[:, 2 * blk + 1:2 * blk + 2]],
                    writes=[zt2[:, :]])
                k.op("s", lambda e: e.activation(thb[:, :], zt2[:, :], AF.Tanh,
                                                 scale=0.5),
                     reads=[zt2[:, :]], writes=[thb[:, :]])
                k.op("v", lambda e: e.tensor_scalar_add(numb[:, :], thb[:, :],
                                                        1.0),
                     reads=[thb[:, :]], writes=[numb[:, :]])
                k.op("v", lambda e: e.tensor_scalar(
                    denb[:, :], thb[:, :], -1.0, 1.0, ALU.mult, ALU.add),
                    reads=[thb[:, :]], writes=[denb[:, :]])
                k.op("v", lambda e: e.reciprocal(denb[:, :], denb[:, :]),
                     reads=[denb[:, :]], writes=[denb[:, :]])
                k.op("v", lambda e: e.tensor_mul(
                    fracb[:, :], numb[:, :], denb[:, :]),
                    reads=[numb[:, :], denb[:, :]], writes=[fracb[:, :]])
                k.op("v", lambda e: e.reduce_sum(ssum1[:, :], fracb[:, :], AX.X),
                     reads=[fracb[:, :]], writes=[ssum1[:, :]])
                k.op("v", lambda e, blk=blk: e.reciprocal(
                    gatem[:, blk:blk + 1], ssum1[:, :]),
                    reads=[ssum1[:, :]], writes=[gatem[:, blk:blk + 1]])
                pe = PB(6)[:, 0:E_EXP]
                oh = onehot[:, blk * E_EXP:(blk + 1) * E_EXP]
                k.op("t", lambda e, pe=pe, oh=oh: e.matmul(
                    pe, Lstrict[:, :], oh, start=True, stop=True),
                    reads=[Lstrict[:, :], oh], writes=[pe], sig=("k", 128, 0))
                k.op("t", lambda e, oh=oh: e.matmul(
                    pbase, onesb[:, :], oh, start=True, stop=True),
                    reads=[onesb[:, :], oh], writes=[pbase], sig=("k", 128, 0))
                k.op("v", lambda e, pe=pe: e.tensor_add(
                    posf[:, :], basef[:, :], pe),
                    reads=[basef[:, :], pe], writes=[posf[:, :]])
                k.op("v", lambda e, oh=oh: e.tensor_mul(
                    selb[:, :], posf[:, :], oh),
                    reads=[posf[:, :], oh], writes=[selb[:, :]])
                k.op("v", lambda e, blk=blk: e.reduce_sum(
                    poss[:, blk:blk + 1], selb[:, :], AX.X),
                    reads=[selb[:, :]], writes=[poss[:, blk:blk + 1]])
                k.op("v", lambda e: e.tensor_add(basef[:, :], basef[:, :],
                                                 pbase),
                     reads=[basef[:, :], pbase], writes=[basef[:, :]])

            def emit_h2ag(ti):
                k.op("g", lambda e, ti=ti: e.collective_compute(
                    "AllGather", ALU.bypass, RG,
                    [h2d[ti * 512:(ti + 1) * 512, :]],
                    [h2ag[ti * n_cores * 512:(ti + 1) * n_cores * 512, :]]),
                    reads=[h2d[ti * 512:(ti + 1) * 512, :]],
                    writes=[h2ag[ti * n_cores * 512:
                                 (ti + 1) * n_cores * 512, :]],
                    chain="hcc%d" % ti, serial=True)

            emitP(0)
            emitP(1)
            emitH1(0)
            emitRMS(0)
            for s in range(NS):
                emitS1a(s)
                if s + 1 < NS:
                    emitH1(s + 1)
                emitS1b(s)
                if s + 1 < NS:
                    emitRMS(s + 1)
                if s >= 1:
                    emitS2(s - 1)
                emitS1c(s)
                if s + 2 < NS:
                    emitP(s + 2)
            emitS2(NS - 1)
            k.op("v", lambda e: e.tensor_copy(hist_sb[:, :], basef[0:1, :]),
                 reads=[basef[:, :]], writes=[hist_sb[:, :]])

        if phase_limit >= 3:
            trashc = sbuf("trashc", [128, 1], F32)
            goffrow = sbuf("goffrow", [1, E_EXP], F32)
            k.op("g", lambda e: e.iota(scr_i[:, 4:5], [[0, 1]],
                                       base=GRID, channel_multiplier=1),
                 writes=[scr_i[:, 4:5]])
            k.op("v", lambda e: e.tensor_copy(trashc[:, :], scr_i[:, 4:5]),
                 reads=[scr_i[:, 4:5]], writes=[trashc[:, :]])

            k.dma("g", cin_h[:, :], hist_sb[:, :], "dci")
            k.op("g", lambda e: e.collective_compute(
                "AllGather", ALU.bypass, RG, [cin_h[:, :]], [cout_h[:, :]]),
                reads=[cin_h[:, :]], writes=[cout_h[:, :]], chain="icc0", serial=True)
            emit_h2ag(NT - 1)
            k.dma("q", histag[:, :], cout_h[:, :], "dcc")
            k.op("t", lambda e: e.matmul(
                PB(6)[0:1, 0:E_EXP], coremask_sb[0:n_cores, 0:1],
                histag[0:n_cores, :], start=True, stop=True),
                reads=[coremask_sb[:, :], histag[:, :]],
                writes=[PB(6)[0:1, 0:E_EXP]], sig=("k", n_cores, 0))
            k.op("v", lambda e: e.tensor_copy(goffrow[:, :], PB(6)[0:1, 0:E_EXP]),
                 reads=[PB(6)[0:1, 0:E_EXP]], writes=[goffrow[:, :]])
            k.op("t", lambda e: e.matmul(
                PB(6)[:, 0:E_EXP], onesf[0:1, 0:128], goffrow[:, :],
                start=True, stop=True),
                reads=[onesf[:, :], goffrow[:, :]], writes=[PB(6)[:, 0:E_EXP]], sig=("k", 1, 0))
            k.op("v", lambda e: e.tensor_copy(goffb[:, :], PB(6)[:, 0:E_EXP]),
                 reads=[PB(6)[:, 0:E_EXP]], writes=[goffb[:, :]])
            posg = TB.view(21696, [128, NBLK], F32)
            accb = TB.view(21728, [128, NBLK], F32)
            for blk in range(NBLK):
                k.op("v", lambda e, blk=blk: e.tensor_mul(
                    selb[:, :], onehot[:, blk * E_EXP:(blk + 1) * E_EXP], goffb[:, :]),
                    reads=[onehot[:, blk * E_EXP:(blk + 1) * E_EXP], goffb[:, :]], writes=[selb[:, :]])
                k.op("v", lambda e: e.reduce_sum(scr_a[:, 0:1], selb[:, :], AX.X),
                     reads=[selb[:, :]], writes=[scr_a[:, 0:1]])
                k.op("v", lambda e, blk=blk: e.tensor_add(
                    scr_a[:, 1:2], scr_a[:, 0:1], poss[:, blk:blk + 1]),
                    reads=[scr_a[:, 0:1], poss[:, blk:blk + 1]],
                    writes=[scr_a[:, 1:2]])
                k.op("v", lambda e: e.tensor_scalar(
                    scr_a[:, 2:3], scr_a[:, 1:2], float(CAP), None, ALU.is_lt),
                    reads=[scr_a[:, 1:2]], writes=[scr_a[:, 2:3]])
                k.op("v", lambda e, blk=blk: e.tensor_copy(
                    posg[:, blk:blk + 1], scr_a[:, 1:2]),
                    reads=[scr_a[:, 1:2]], writes=[posg[:, blk:blk + 1]])
                k.op("v", lambda e, blk=blk: e.tensor_copy(
                    accb[:, blk:blk + 1], scr_a[:, 2:3]),
                    reads=[scr_a[:, 2:3]], writes=[accb[:, blk:blk + 1]])
                k.op("v", lambda e, blk=blk: e.tensor_scalar(
                    scr_a[:, 3:4], top1[:, blk:blk + 1], float(CAP), scr_a[:, 1:2],
                    ALU.mult, ALU.add),
                    reads=[top1[:, blk:blk + 1], scr_a[:, 1:2]],
                    writes=[scr_a[:, 3:4]])
                k.op("v", lambda e: e.tensor_mul(
                    scr_b[:, 1:2], scr_a[:, 3:4], scr_a[:, 2:3]),
                    reads=[scr_a[:, 2:4]], writes=[scr_b[:, 1:2]])
                k.op("v", lambda e: e.tensor_scalar(
                    scr_b[:, 2:3], scr_a[:, 2:3], -1.0, 1.0, ALU.mult, ALU.add),
                    reads=[scr_a[:, 2:3]], writes=[scr_b[:, 2:3]])
                k.op("v", lambda e: e.tensor_mul(
                    scr_b[:, 2:3], scr_b[:, 2:3], trashc[:, :]),
                    reads=[scr_b[:, 2:3], trashc[:, :]], writes=[scr_b[:, 2:3]])
                k.op("v", lambda e: e.tensor_add(
                    scr_b[:, 1:2], scr_b[:, 1:2], scr_b[:, 2:3]),
                    reads=[scr_b[:, 1:3]], writes=[scr_b[:, 1:2]])
                k.op("v", lambda e, blk=blk: e.tensor_copy(
                    slotu[:, blk:blk + 1], scr_b[:, 1:2]),
                    reads=[scr_b[:, 1:2]], writes=[slotu[:, blk:blk + 1]])
            invd_pa = invd.rearrange("(p a) -> p a", p=128)
            invd_rows = invd.rearrange("(a one) -> a one", one=1)
            for z0 in range(0, INVROWS, 128):
                zn = min(128, INVROWS - z0)
                k.dma("g", invd_pa[:, z0:z0 + zn], zerot[:, 0:zn], "dsc")
            for blk in range(NBLK):
                k.op("g", lambda e, blk=blk: e.indirect_dma_start(
                    invd_rows,
                    IndirectOffsetOnAxis(ap=slotu[:, blk:blk + 1], axis=0),
                    tokid_sb[:, blk:blk + 1], None),
                    reads=[tokid_sb[:, blk:blk + 1], slotu[:, blk:blk + 1]],
                    writes=[invd_rows], dma=True, dma_chain="dsc%d" % (blk % 4))
            k.op("g", lambda e: e.collective_compute(
                "AllReduce", ALU.add, RG, [invd[:]], [invr[:]]),
                reads=[invd[:]], writes=[invr[:]], chain="icc1", serial=True)
            k.op("g", lambda e: e.load_library(library_config.mlp),
                 chain="lib", serial=True)
            invmy = TB.view(8192, [128, 1, 128], F32)
            invr_rows = invr.rearrange("(r c) -> r c", c=128)
            k.op("g", lambda e: e.dma_gather(
                invmy, invr_rows, myslots_sb[:, :], NSB, NSB, 128),
                reads=[invr_rows, myslots_sb[:, :]], writes=[invmy],
                dma=True, dma_chain="dsc")
            invmy2 = TB.view(8192, [128, 128], F32)
            k.op("t", lambda e: e.transpose(
                ptf[0][:, 0:NSB], invmy2[0:NSB, :], identf[0:NSB, 0:NSB]),
                reads=[invmy2, identf[:, :]], writes=[ptf[0]], sig=("k", 128, 0))
            k.op("v", lambda e: e.tensor_copy(invT[:, 0:NSB], ptf[0][:, 0:NSB]),
                 reads=[ptf[0]], writes=[invT[:, 0:NSB]])

            bufg = [TB.view(o, [128, D], BF16)
                    for o in (0, 512, 8320, 8832, 9344, 9856, 10368, 10880,
                              13440, 13952, 14464, 14976, 15488, 16000,
                              16512, 17024)]
            buffm = [TB.view(1024, [128, 8, 512], BF16),
                     TB.view(11392, [128, 8, 512], BF16)]
            gfm = TB.view(3072, [128, 8, 512], BF16)
            ufm = TB.view(5120, [128, 8, 512], BF16)
            ystage = [TB.view(7168, [128, D], BF16), TB.view(7680, [128, D], BF16)]
            wslot = [WA.view(4096 * i, [128, 8, D], BF16) for i in range(4)]

            def load_mat(slot, src, e_loc, chain):
                for kk in range(8):
                    k.dma("g", wslot[slot][:, kk, :],
                          src[e_loc, kk * 128:(kk + 1) * 128, :], chain)

            groups = []
            for e_loc in range(EPC):
                b0 = e_loc * BPE
                rem = BPE
                while rem > 0:
                    n = min(4, rem)
                    groups.append((e_loc, b0 + (BPE - rem), n))
                    rem -= n
            load_mat(0, wgate_in, 0, "dw0")
            load_mat(1, wup_in, 0, "dw1")
            load_mat(2, wdown_in, 0, "dw0")
            nextslot = 3
            loaded_e = 0

            def emit_gathers(gi):
                _, sb0_g, n_g = groups[gi]
                for j in range(n_g):
                    sb = sb0_g + j
                    bg = bufg[sb % 16]
                    k.op("g", lambda e, bg=bg, sb=sb: e.indirect_dma_start(
                        bg, None, h2ag[:, :],
                        IndirectOffsetOnAxis(ap=invT[:, sb:sb + 1], axis=0)),
                        reads=[h2ag[:, :], invT[:, sb:sb + 1]], writes=[bg],
                        dma=True, dma_chain="dgh%d" % (sb % 4))

            gq = [0]

            def pump_gathers(upto):
                while gq[0] < min(upto, len(groups)):
                    emit_gathers(gq[0])
                    gq[0] += 1

            pump_gathers(1)
            yag_next = [0]
            for gi, (e_loc, sb0, n) in enumerate(groups):
                sg = wslot[(3 * e_loc) % 4]
                su = wslot[(3 * e_loc + 1) % 4]
                sd = wslot[(3 * e_loc + 2) % 4]
                first_of_e = (gi == 0 or groups[gi - 1][0] != e_loc)
                last_of_e = (gi + 1 == len(groups) or groups[gi + 1][0] != e_loc)
                if first_of_e and e_loc + 1 < EPC:
                    load_mat((3 * (e_loc + 1)) % 4, wgate_in, e_loc + 1, "dw1")
                pump_gathers(gi + 4)
                bfm = buffm[gi % 2]
                N = n * 128
                for j in range(n):
                    sb = sb0 + j
                    bg = bufg[sb % 16]
                    for f in range(8):
                        pt = ptp[f % 4]
                        k.op("t", lambda e, bg=bg, f=f, pt=pt: e.transpose(
                            pt, bg[:, f * 128:(f + 1) * 128], identb[:, :]),
                            reads=[bg, identb[:, :]], writes=[pt], sig=("k", 128, 0))
                        dst = bfm[:, f, j * 128:(j + 1) * 128]
                        eng = "v" if f % 2 else "s"
                        k.op(eng, lambda e, dst=dst, pt=pt: (
                            e.tensor_copy(dst, pt) if hasattr(e, "tensor_copy")
                            else e.copy(dst, pt)),
                            reads=[pt], writes=[dst])
                for m in range(8):
                    pq = PB(m % 4)
                    for kk in range(8):
                        k.op("t", lambda e, m=m, kk=kk, pq=pq, sg=sg, N=N, bfm=bfm: e.matmul(
                            pq[:, 0:N], sg[:, kk, m * 128:(m + 1) * 128],
                            bfm[:, kk, 0:N], start=(kk == 0), stop=(kk == 7)),
                            reads=[sg[:, kk, :], bfm[:, kk, 0:N]],
                            writes=[pq[:, 0:N]], inc=(kk == 7), sig=("k", 128, 0))
                    k.op("s", lambda e, m=m, pq=pq, N=N: e.activation(
                        gfm[:, m, 0:N], pq[:, 0:N], AF.Silu),
                        reads=[pq[:, 0:N]], writes=[gfm[:, m, 0:N]])
                if last_of_e and e_loc + 1 < EPC:
                    load_mat((3 * (e_loc + 1) + 1) % 4, wup_in, e_loc + 1, "dw0")
                for m in range(8):
                    pq = PB(m % 4)
                    for kk in range(8):
                        k.op("t", lambda e, m=m, kk=kk, pq=pq, su=su, N=N, bfm=bfm: e.matmul(
                            pq[:, 0:N], su[:, kk, m * 128:(m + 1) * 128],
                            bfm[:, kk, 0:N], start=(kk == 0), stop=(kk == 7)),
                            reads=[su[:, kk, :], bfm[:, kk, 0:N]],
                            writes=[pq[:, 0:N]], inc=(kk == 7), sig=("k", 128, 0))
                    k.op("v", lambda e, m=m, pq=pq, N=N: e.tensor_copy(
                        ufm[:, m, 0:N], pq[:, 0:N]),
                        reads=[pq[:, 0:N]], writes=[ufm[:, m, 0:N]])
                    k.op("v", lambda e, m=m, N=N: e.tensor_mul(
                        gfm[:, m, 0:N], gfm[:, m, 0:N], ufm[:, m, 0:N]),
                        reads=[gfm[:, m, 0:N], ufm[:, m, 0:N]],
                        writes=[gfm[:, m, 0:N]])
                if last_of_e and e_loc + 1 < EPC:
                    load_mat((3 * (e_loc + 1) + 2) % 4, wdown_in, e_loc + 1, "dw1")
                for ts in range(n):
                    py = PB(4, 2) if ts % 2 == 0 else PB(6, 2)
                    for nh in range(2):
                        for f in range(8):
                            k.op("t", lambda e, ts=ts, f=f, nh=nh, py=py, sd=sd:
                                 e.matmul(
                                     py[:, nh * 512:(nh + 1) * 512],
                                     gfm[:, f, ts * 128:(ts + 1) * 128],
                                     sd[:, f, nh * 512:(nh + 1) * 512],
                                     start=(f == 0), stop=(f == 7)),
                                 reads=[gfm[:, f, :], sd[:, f, :]],
                                 writes=[py[:, nh * 512:(nh + 1) * 512]],
                                 inc=(f == 7), sig=("k", 128, 0))
                    ys = ystage[ts % 2]
                    k.op("v", lambda e, ys=ys, py=py: e.tensor_copy(ys, py),
                         reads=[py], writes=[ys])
                    k.dma("g", yin[(sb0 + ts) * 128:(sb0 + ts + 1) * 128, :], ys,
                          "dys")
                stored_e = (sb0 + n - e_loc * BPE) * 128
                while yag_next[0] < 2 * EPC:
                    e_ag, hf = yag_next[0] // 2, yag_next[0] % 2
                    ready = e_ag < e_loc or (
                        e_ag == e_loc and (hf + 1) * HCAP <= stored_e)
                    if not ready:
                        break
                    i0 = e_ag * CAP + hf * HCAP
                    o0 = (e_ag * 2 + hf) * n_cores * HCAP
                    k.op("g", lambda e_, i0=i0, o0=o0: e_.collective_compute(
                        "AllGather", ALU.bypass, RG,
                        [yin[i0:i0 + HCAP, :]],
                        [yag[o0:o0 + n_cores * HCAP, :]]),
                        reads=[yin[i0:i0 + HCAP, :]],
                        writes=[yag[o0:o0 + n_cores * HCAP, :]],
                        chain="ycc%d" % (e_ag * 2 + hf), serial=True)
                    yag_next[0] += 1

            for blk in range(NBLK):
                k.op("v", lambda e, blk=blk: e.tensor_mul(
                    gatem[:, blk:blk + 1], gatem[:, blk:blk + 1],
                    accb[:, blk:blk + 1]),
                    reads=[gatem[:, blk:blk + 1], accb[:, blk:blk + 1]],
                    writes=[gatem[:, blk:blk + 1]])
                k.op("v", lambda e, blk=blk: e.tensor_copy(
                    scr_i[:, 8:9], top1[:, blk:blk + 1]),
                    reads=[top1[:, blk:blk + 1]], writes=[scr_i[:, 8:9]])
                k.op("v", lambda e: e.tensor_scalar(
                    scr_i[:, 9:10], scr_i[:, 8:9], EPC - 1, None,
                    ALU.bitwise_and),
                    reads=[scr_i[:, 8:9]], writes=[scr_i[:, 9:10]])
                k.op("v", lambda e: e.tensor_copy(scr_b[:, 4:5], scr_i[:, 9:10]),
                     reads=[scr_i[:, 9:10]], writes=[scr_b[:, 4:5]])
                k.op("v", lambda e, blk=blk: e.tensor_scalar(
                    scr_b[:, 5:6], top1[:, blk:blk + 1], scr_b[:, 4:5],
                    1.0 / EPC, ALU.subtract, ALU.mult),
                    reads=[top1[:, blk:blk + 1], scr_b[:, 4:5]],
                    writes=[scr_b[:, 5:6]])
                k.op("v", lambda e, blk=blk: e.tensor_scalar(
                    scr_b[:, 9:10], posg[:, blk:blk + 1], float(HCAP), None,
                    ALU.is_ge),
                    reads=[posg[:, blk:blk + 1]], writes=[scr_b[:, 9:10]])
                k.op("v", lambda e, blk=blk: e.tensor_scalar(
                    scr_b[:, 10:11], scr_b[:, 9:10], -float(HCAP),
                    posg[:, blk:blk + 1], ALU.mult, ALU.add),
                    reads=[scr_b[:, 9:10], posg[:, blk:blk + 1]],
                    writes=[scr_b[:, 10:11]])
                k.op("v", lambda e: e.tensor_scalar(
                    scr_b[:, 6:7], scr_b[:, 9:10], float(n_cores * HCAP),
                    scr_b[:, 10:11], ALU.mult, ALU.add),
                    reads=[scr_b[:, 9:11]], writes=[scr_b[:, 6:7]])
                k.op("v", lambda e: e.tensor_scalar(
                    scr_b[:, 7:8], scr_b[:, 4:5], float(2 * n_cores * HCAP),
                    scr_b[:, 6:7], ALU.mult, ALU.add),
                    reads=[scr_b[:, 4:7]], writes=[scr_b[:, 7:8]])
                k.op("v", lambda e: e.tensor_scalar(
                    scr_b[:, 7:8], scr_b[:, 5:6], float(HCAP), scr_b[:, 7:8],
                    ALU.mult, ALU.add),
                    reads=[scr_b[:, 5:8]], writes=[scr_b[:, 7:8]])
                k.op("v", lambda e, blk=blk: e.tensor_mul(
                    scr_b[:, 7:8], scr_b[:, 7:8], accb[:, blk:blk + 1]),
                    reads=[scr_b[:, 7:8], accb[:, blk:blk + 1]],
                    writes=[scr_b[:, 7:8]])
                k.op("v", lambda e, blk=blk: e.tensor_scalar(
                    scr_b[:, 8:9], accb[:, blk:blk + 1], -float(GRID - 1),
                    float(GRID - 1), ALU.mult, ALU.add),
                    reads=[accb[:, blk:blk + 1]], writes=[scr_b[:, 8:9]])
                k.op("v", lambda e: e.tensor_add(
                    scr_b[:, 7:8], scr_b[:, 7:8], scr_b[:, 8:9]),
                    reads=[scr_b[:, 7:9]], writes=[scr_b[:, 7:8]])
                k.op("v", lambda e, blk=blk: e.tensor_copy(
                    slotcl[:, blk:blk + 1], scr_b[:, 7:8]),
                    reads=[scr_b[:, 7:8]], writes=[slotcl[:, blk:blk + 1]])

        if phase_limit >= 4:
            yg = [TB.view(13440 + 512 * i, [128, D], BF16) for i in range(4)]
            x1r = [TB.view(15488 + 1024 * i, [128, D]) for i in range(4)]
            h2f = [TB.view(19584, [128, D]), TB.view(20608, [128, D]),
                   TB.view(0, [128, D]), TB.view(1024, [128, D])]
            fbuf = [TB.view(2048, [128, D]), TB.view(3072, [128, D])]
            tbufF = [TB.view(4096, [128, D]), TB.view(5120, [128, D])]
            outb = [TB.view(6144 + 1024 * i, [128, D]) for i in range(4)]
            def finpre(blk):
                p4 = blk % 4
                k.op("g", lambda e, p4=p4, blk=blk: e.indirect_dma_start(
                    yg[p4], None, yag[:, :],
                    IndirectOffsetOnAxis(ap=slotcl[:, blk:blk + 1], axis=0)),
                    reads=[yag[:, :], slotcl[:, blk:blk + 1]], writes=[yg[p4]],
                    dma=True, dma_chain="dyg%d" % p4)
                k.dma("q", x1r[p4], x1d[blk * 128:(blk + 1) * 128, :],
                      "dxf%d" % p4)
                k.op("s", lambda e, p4=p4, blk=blk: e.activation(
                    h2f[p4], x1r[p4], AF.Identity,
                    bias=m2r2[:, 2 * blk:2 * blk + 1],
                    scale=m2r2[:, 2 * blk + 1:2 * blk + 2]),
                    reads=[x1r[p4], m2r2[:, 2 * blk:2 * blk + 1], m2r2[:, 2 * blk + 1:2 * blk + 2]],
                    writes=[h2f[p4]])

            def fin_c2(blk):
                p4 = blk % 4
                p2 = blk % 2
                k.op("v", lambda e, p4=p4, p2=p2: e.tensor_add(
                    outb[p4], x1r[p4], tbufF[p2]),
                    reads=[x1r[p4], tbufF[p2]], writes=[outb[p4]])
                k.dma("s", y_out[blk * 128:(blk + 1) * 128, :], outb[p4],
                      "dout%d" % p2)

            finpre(0)
            finpre(1)
            for blk in range(NBLK):
                p4 = blk % 4
                p2 = blk % 2
                if blk + 2 < NBLK:
                    finpre(blk + 2)
                k.op("v", lambda e, p4=p4, p2=p2, blk=blk: e.scalar_tensor_tensor(
                    fbuf[p2], yg[p4], gatem[:, blk:blk + 1], h2f[p4],
                    ALU.mult, ALU.add),
                    reads=[yg[p4], gatem[:, blk:blk + 1], h2f[p4]],
                    writes=[fbuf[p2]])
                ln_stats(fbuf[p2], scr_a[:, 8 + 2 * p2:9 + 2 * p2],
                         scr_a[:, 9 + 2 * p2:10 + 2 * p2])
                k.op("s", lambda e, p2=p2: e.activation(
                    tbufF[p2], fbuf[p2], AF.Identity,
                    bias=scr_a[:, 8 + 2 * p2:9 + 2 * p2],
                    scale=scr_a[:, 9 + 2 * p2:10 + 2 * p2]),
                    reads=[fbuf[p2], scr_a[:, 8 + 2 * p2:10 + 2 * p2]],
                    writes=[tbufF[p2]])
                if blk >= 1:
                    fin_c2(blk - 1)
            fin_c2(NBLK - 1)
            k.wait("g", "dout0", k.barrier_value("dout0"))
            k.wait("g", "dout1", k.barrier_value("dout1"))

        with nc.Block() as block:
            k.finalize(block)
    mybir.codegen_inst_isa_subclasses(nc)
    return nc



def _stage_inputs(inputs, n_cores=8, cap=None):
    L = inputs["L0_attn_hidden_states"].shape[1]
    NBLK = L // 128
    T_tot = n_cores * L
    CAP = cap if cap is not None else int(math.ceil(CF * T_tot / E_EXP))
    EPC = E_EXP // n_cores
    NSB = EPC * CAP // 128
    x = np.ascontiguousarray(np.asarray(inputs["L0_attn_hidden_states"],
                                        dtype=np.float32))
    com = {
        "wq": np.ascontiguousarray(inputs["L0_attn_Wq"], dtype=np.float32),
        "wk": np.ascontiguousarray(inputs["L0_attn_Wk"], dtype=np.float32),
        "wv": np.ascontiguousarray(inputs["L0_attn_Wv"], dtype=np.float32),
        "wg": np.ascontiguousarray(inputs["L0_attn_Wg"], dtype=np.float32),
        "wgk1": np.ascontiguousarray(inputs["L0_attn_Wgk1"], dtype=np.float32),
        "wgk2": np.ascontiguousarray(inputs["L0_attn_Wgk2"], dtype=np.float32),
        "bgk2": np.ascontiguousarray(
            np.asarray(inputs["L0_attn_bgk2"], dtype=np.float32).reshape(1, -1)),
        "gnw": np.ascontiguousarray(
            np.asarray(inputs["L0_attn_g_norm_w"], dtype=np.float32).reshape(1, -1)),
        "wo": np.ascontiguousarray(inputs["L0_attn_Wo"], dtype=np.float32),
        "wr": np.ascontiguousarray(inputs["L0_ffn_w_router"], dtype=np.float32),
    }
    wgate = np.ascontiguousarray(inputs["L0_ffn_w_gate"], dtype=np.float32)
    wup = np.ascontiguousarray(inputs["L0_ffn_w_up"], dtype=np.float32)
    wdown = np.ascontiguousarray(inputs["L0_ffn_w_down"], dtype=np.float32)
    in_maps = []
    for c in range(n_cores):
        m = dict(com)
        m["x"] = x[c]
        m["wgate"] = wgate[c * EPC:(c + 1) * EPC]
        m["wup"] = wup[c * EPC:(c + 1) * EPC]
        m["wdown"] = wdown[c * EPC:(c + 1) * EPC]
        cm = np.zeros((8, 1), dtype=np.float32)
        cm[:c] = 1.0
        m["coremask"] = cm
        blks = np.arange(NBLK)[None, :]
        tok = ((blks // 4) * (n_cores * 512) + c * 512 + (blks % 4) * 128
               + np.arange(128)[:, None]).astype(np.float32)
        m["tokid"] = np.ascontiguousarray(tok)
        ncol = -(-NSB // 16)
        ms = np.full((128, ncol), -1, dtype=np.int16)
        base = c * NSB
        for i in range(NSB):
            for g in range(8):
                ms[i % 16 + 16 * g, i // 16] = base + i
        m["myslots"] = ms
        in_maps.append(m)
    return in_maps


def kernel(**inputs):
    n_cores = 8
    L = inputs["L0_attn_hidden_states"].shape[1]
    in_maps = _stage_inputs(inputs, n_cores)
    nc = build_core(L=L, n_cores=n_cores)
    res = run_bass_kernel_spmd(nc, in_maps, core_ids=list(range(n_cores)))
    return np.stack([res.results[c]["y"] for c in range(n_cores)], axis=0)


# version 24
# speedup vs baseline: 1.0166x; 1.0166x over previous
import math
from contextlib import ExitStack

import numpy as np

import concourse.bass as bass
import concourse.mybir as mybir
from concourse.bass_utils import run_bass_kernel_spmd
from concourse.bass_core_rust import IndirectOffsetOnAxis
from concourse import library_config

F32 = mybir.dt.float32
BF16 = mybir.dt.bfloat16
U32 = mybir.dt.uint32
I32 = mybir.dt.int32
I16 = mybir.dt.int16
AF = mybir.ActivationFunctionType
ALU = mybir.AluOpType
AX = mybir.AxisListType


ENGS = ("t", "v", "s", "g", "q")
DMA_CHAINS = ("dq", "dg", "ds")
PSUM_BANK = 2048


def _ap_extent(ap):
    t = ap.tensor
    esz = mybir.dt.size(ap.dtype)
    dims = list(ap.ap)
    off = ap.offset if isinstance(ap.offset, int) else 0
    from concourse.bass import MemorySpace
    sp = ap.space
    if sp in (MemorySpace.SBUF, MemorySpace.PSUM):
        base_p = getattr(t, "base_partition", 0) or 0
        pstep, pcount = dims[0]
        if pstep > 0:
            p_off, col = off // pstep, off % pstep
        else:
            p_off, col = 0, off
        plo, phi = base_p + p_off, base_p + p_off + pcount
        lo, hi = 0, 1
        for step, count in dims[1:]:
            if step >= 0:
                hi += step * (count - 1)
            else:
                lo += step * (count - 1)
        blo, bhi = (col + lo) * esz, (col + hi) * esz
        if sp == MemorySpace.PSUM:
            blo = (blo // PSUM_BANK) * PSUM_BANK
            bhi = -(-bhi // PSUM_BANK) * PSUM_BANK
            plo, phi = 0, 128
        return (t.name, plo, phi, blo, bhi)
    lo, hi = 0, 1
    for step, count in dims:
        if step >= 0:
            hi += step * (count - 1)
        else:
            lo += step * (count - 1)
    return (t.name, 0, 128, (off + lo) * esz, (off + hi) * esz)


def _overlap(a, b):
    return (a[0] == b[0] and a[1] < b[2] and b[1] < a[2]
            and a[3] < b[4] and b[3] < a[4])


def _covers(a, b):
    return (a[0] == b[0] and a[1] <= b[1] and a[2] >= b[2]
            and a[3] <= b[3] and a[4] >= b[4])


class K:
    def __init__(self, nc, stack: ExitStack):
        self.nc = nc
        self.stack = stack
        self.chain = {}
        self.cnt = {}
        for e in ENGS + DMA_CHAINS:
            self.new_chain(e)
        self.obs = {e: {} for e in ENGS}
        self.clock_at = {}
        self.writes = {}
        self.reads = {}
        self.prog = {e: [] for e in ENGS}
        self.last_tsig = None

    def new_chain(self, name):
        if name not in self.chain:
            self.chain[name] = self.stack.enter_context(
                self.nc.semaphore(f"chain_{name}"))
            self.cnt[name] = 0
        return name

    def _need_waits(self, e, raps, waps):
        need = {}
        for ap in raps:
            ext = _ap_extent(ap)
            for (wext, ch, c) in self.writes.get(ext[0], ()):
                if _overlap(ext, wext):
                    need[ch] = max(need.get(ch, 0), c)
        for ap in waps:
            ext = _ap_extent(ap)
            for (wext, ch, c) in self.writes.get(ext[0], ()):
                if _overlap(ext, wext):
                    need[ch] = max(need.get(ch, 0), c)
            for (rext, ch, c) in self.reads.get(ext[0], ()):
                if _overlap(ext, rext):
                    need[ch] = max(need.get(ch, 0), c)
        out = {}
        for ch, c in need.items():
            if ch == e and c > self.cnt[e]:
                continue
            out[ch] = c
        return out

    def _observe(self, e, ch, c):
        obs = self.obs[e]
        if obs.get(ch, 0) >= c:
            return
        obs[ch] = c
        snap = self.clock_at.get((ch, c))
        if snap:
            for ch2, c2 in snap.items():
                if obs.get(ch2, 0) < c2:
                    obs[ch2] = c2

    def op(self, e, emit, reads=(), writes=(), inc=True, dma=False,
           dma_chain=None, extra_waits=None, chain=None, step=None,
           serial=False, sig=None):
        if e == "t" and sig is not None:
            if self.last_tsig is not None and sig != self.last_tsig:
                self.prog["t"].append(({}, lambda eng: eng.drain(), False, 0, "t"))
            self.last_tsig = sig
        if dma:
            chain, step, serial = dma_chain, 16, True
        ch = chain if chain is not None else e
        step = step if step is not None else 1
        waits = self._need_waits(e, reads, writes)
        if serial and self.cnt[ch] > 0:
            waits[ch] = max(waits.get(ch, 0), self.cnt[ch])
        if extra_waits:
            for wch, wc in extra_waits.items():
                waits[wch] = max(waits.get(wch, 0), wc)
        waits = {wch: wc for wch, wc in waits.items()
                 if self.obs[e].get(wch, 0) < wc}
        for wch, wc in sorted(waits.items()):
            self._observe(e, wch, wc)
        if inc:
            self.cnt[ch] += step
            snap = dict(self.obs[e])
            snap[e] = self.cnt[e]
            snap[ch] = self.cnt[ch]
            self.clock_at[(ch, self.cnt[ch])] = snap
            c_rec = self.cnt[ch]
        else:
            c_rec = self.cnt[ch] + step
        self.prog[e].append((waits, emit, inc, step, ch))
        for ap in writes:
            ext = _ap_extent(ap)
            lst = self.writes.setdefault(ext[0], [])
            lst[:] = [w for w in lst if not (w[1] == ch and w[2] <= c_rec
                                             and _covers(ext, w[0]))]
            lst.append((ext, ch, c_rec))
            rl = self.reads.get(ext[0])
            if rl:
                rl[:] = [r for r in rl if not _covers(ext, r[0])]
        for ap in reads:
            ext = _ap_extent(ap)
            self.reads.setdefault(ext[0], []).append((ext, ch, c_rec))

    def dma(self, e, out, in_, chain, **kw):
        self.op(e, lambda eng: eng.dma_start(out=out, in_=in_, **kw),
                reads=[in_], writes=[out], dma=True, dma_chain=chain)

    def barrier_value(self, ch):
        return self.cnt[ch]

    def wait(self, e, ch=None, c=None):
        if ch is None:
            return
        if self.obs[e].get(ch, 0) >= c:
            return
        self._observe(e, ch, c)
        self.prog[e].append(({ch: c}, None, False, 0, e))

    def finalize(self, block):
        name_map = {"t": "tensor", "v": "vector", "s": "scalar",
                    "g": "gpsimd", "q": "sync"}
        for e in ENGS:
            plist = self.prog[e]
            if not plist:
                continue
            def run_all(engine, plist=plist):
                for (waits, emit, inc, step, ch) in plist:
                    for wch, wc in sorted(waits.items()):
                        engine.wait_ge(self.chain[wch], wc)
                    if emit is None:
                        continue
                    inst = emit(engine)
                    if inc:
                        if inst is None:
                            inst = engine.nop()
                        inst.then_inc(self.chain[ch], step)
            getattr(block, name_map[e])(run_all)


D = 1024
NH, DK, DV = 4, 1024, 2048
HDK, HDV = 256, 512
LOWRANK = 16
GATE_NORM = 16.0
CHUNK = 64
EPS = 1e-5
E_EXP, CF = 32, 1.25
LN16 = math.log(16.0)


def quake_rsqrt(k, out, in_, tmp_i, tmp_b, e="v", iters=2):
    k.op(e, lambda v: v.tensor_scalar(
        tmp_i, in_.bitcast(I32), 1, None, ALU.logical_shift_right),
        reads=[in_], writes=[tmp_i])
    k.op(e, lambda v: v.tensor_scalar(
        tmp_i, tmp_i, -1.0, float(0x5F3759DF), ALU.mult, ALU.add),
        reads=[tmp_i], writes=[tmp_i])
    y = out
    k.op(e, lambda v: v.tensor_copy(y, tmp_i.bitcast(F32)),
         reads=[tmp_i], writes=[y])
    for _ in range(iters):
        k.op(e, lambda v: v.tensor_mul(tmp_b, y, y), reads=[y], writes=[tmp_b])
        k.op(e, lambda v: v.tensor_mul(tmp_b, tmp_b, in_),
             reads=[tmp_b, in_], writes=[tmp_b])
        k.op(e, lambda v: v.tensor_scalar(
            tmp_b, tmp_b, -0.5, 1.5, ALU.mult, ALU.add),
            reads=[tmp_b], writes=[tmp_b])
        k.op(e, lambda v: v.tensor_mul(y, y, tmp_b),
             reads=[y, tmp_b], writes=[y])


def kernel(**inputs):
    raise NotImplementedError


class Arena:

    def __init__(self, nc, name, cols_f32):
        self.t = nc.alloc_sbuf_tensor(name, [128, cols_f32], F32)
        self.cols = cols_f32

    def view(self, off, shape, dtype=F32):
        n = 1
        for s in shape[1:]:
            n *= s
        nf = (n * mybir.dt.size(dtype) + 3) // 4
        assert off + nf <= self.cols, (off, nf, self.cols)
        ap = self.t[:, off:off + nf]
        if dtype != F32:
            ap = ap.bitcast(dtype)
        ap = ap[:, :n]
        if len(shape) == 3:
            ap = ap.rearrange("p (a b) -> p a b", a=shape[1])
        if shape[0] < 128:
            ap = ap[0:shape[0]]
        return ap


def ceil_div(a, b):
    return -(-a // b)


def build_core(L=4096, n_cores=8, cap=None, probe=None, phase_limit=4, pa=4):
    NT = L // 512
    NSUB = 4
    NBLK = L // 128
    T_tot = n_cores * L
    CAP = cap if cap is not None else int(math.ceil(CF * T_tot / E_EXP))
    EPC = E_EXP // n_cores
    SLOTS = EPC * CAP
    assert SLOTS % 128 == 0 and CAP % 128 == 0
    NSB = SLOTS // 128
    BPE = CAP // 128
    HCAP = CAP // 2
    assert CAP % 2 == 0
    GRID = E_EXP * CAP
    INVN = GRID + 128
    INVROWS = ceil_div(INVN, 128)
    RG = [list(range(n_cores))]

    nc = bass.Bass("TRN2", target_bir_lowering=False, debug=False)

    din = {}
    def inp(name, shape, dt=F32):
        din[name] = nc.dram_tensor(name, shape, dt, kind="ExternalInput").ap()
        return din[name]
    x_in = inp("x", [L, D])
    wq_in = inp("wq", [D, DK]); wk_in = inp("wk", [D, DK])
    wv_in = inp("wv", [D, DV]); wg_in = inp("wg", [D, DV])
    wgk1_in = inp("wgk1", [D, LOWRANK]); wgk2_in = inp("wgk2", [LOWRANK, DK])
    bgk2_in = inp("bgk2", [1, DK]); gnw_in = inp("gnw", [1, HDV])
    wo_in = inp("wo", [DV, D]); wr_in = inp("wr", [D, E_EXP])
    wgate_in = inp("wgate", [EPC, D, D]); wup_in = inp("wup", [EPC, D, D])
    wdown_in = inp("wdown", [EPC, D, D])
    coremask_in = inp("coremask", [8, 1])
    tokid_in = inp("tokid", [128, NBLK])
    myslots_in = inp("myslots", [128, ceil_div(NSB, 16)], I16)

    y_out = nc.dram_tensor("y", [L, D], F32, kind="ExternalOutput").ap()

    od = nc.dram_tensor("od", [L, DV], BF16, kind="Internal").ap()
    x1d = nc.dram_tensor("x1d", [L, D], F32, kind="Internal").ap()
    cin_h = nc.dram_tensor("cin_h", [1, E_EXP], F32, kind="Internal").ap()
    cout_h = nc.dram_tensor("cout_h", [n_cores, E_EXP], F32, kind="Internal").ap()
    invd = nc.dram_tensor("invd", [INVROWS * 128], F32, kind="Internal").ap()
    invr = nc.dram_tensor("invr", [INVROWS * 128], F32, kind="Internal").ap()
    h2d = nc.dram_tensor("h2d", [L, D], BF16, kind="Internal").ap()
    h2ag = nc.dram_tensor("h2ag", [T_tot, D], BF16, kind="Internal").ap()
    yin = nc.dram_tensor("yin", [SLOTS, D], BF16, kind="Internal").ap()
    yag = nc.dram_tensor("yag", [GRID, D], BF16, kind="Internal").ap()

    with ExitStack() as st:
        k = K(nc, st)
        for ch in ("dx0", "dx1", "dw0", "dw1", "dos0", "dos1", "dor0", "dor1",
                   "dx1s", "dh2s", "dcc", "dci", "cc", "lib", "dsc",
                   "dsc0", "dsc1", "dsc2", "dsc3",
                   "dgh0", "dgh1", "dgh2", "dgh3",
                   "dys", "dyg0", "dyg1", "dyg2", "dyg3",
                   "dxf0", "dxf1", "dxf2", "dxf3", "dout0", "dout1",
                   "icc0", "icc1"):
            k.new_chain(ch)
        for ti_ in range(NT):
            k.new_chain("hcc%d" % ti_)
        for eh_ in range(2 * EPC):
            k.new_chain("ycc%d" % eh_)

        sbuf = lambda name, shape, dt=F32: nc.alloc_sbuf_tensor(
            name, shape, dt)

        S = sbuf("S", [128, 2 * NH, HDV])
        identb = sbuf("identb", [128, 128], BF16)
        identf = sbuf("identf", [128, 128], F32)
        onesb = sbuf("onesb", [128, 128], BF16)
        Lstrict = sbuf("Lstrict", [128, 128], BF16)
        L64p = sbuf("L64p", [128, 2, 64], F32)
        Lshp = sbuf("Lshp", [128, 2, 64], F32)
        maskT = sbuf("maskT", [128, 64], F32)
        onesf = sbuf("onesf", [128, 128], F32)
        gnwb = sbuf("gnwb", [128, DV], BF16)
        wgk2x = sbuf("wgk2x", [128, DK], BF16)
        wgk1b = sbuf("wgk1b", [128, 8, LOWRANK], BF16)
        wrf = sbuf("wrf", [128, 8, E_EXP], F32)
        colr = sbuf("colr", [1, E_EXP], F32)
        t1s = sbuf("t1s", [128, 512], BF16)
        gnw_st = sbuf("gnw_st", [1, HDV], F32)
        bgk2_st = sbuf("bgk2_st", [1, DK], F32)
        bgk2_hf = sbuf("bgk2_hf", [1, DK], F32)
        coremask_sb = sbuf("coremask_sb", [8, 1], F32)
        tokid_sb = sbuf("tokid_sb", [128, NBLK], F32)
        myslots_sb = sbuf("myslots_sb", [128, ceil_div(NSB, 16)], I16)
        histag = sbuf("histag", [n_cores, E_EXP], F32)
        goffb = sbuf("goffb", [128, E_EXP], F32)
        m1r1 = sbuf("m1r1", [128, NBLK * 2], F32)
        m2r2 = sbuf("m2r2", [128, NBLK * 2], F32)
        r2s = sbuf("r2s", [128, NBLK], F32)
        nm2s = sbuf("nm2s", [128, NBLK], F32)
        top1 = sbuf("top1", [128, NBLK], F32)
        gatem = sbuf("gatem", [128, NBLK], F32)
        poss = sbuf("poss", [128, NBLK], F32)
        onehot = sbuf("onehot", [128, NBLK * E_EXP], BF16)
        slotu = sbuf("slotu", [128, NBLK], U32)
        slotcl = sbuf("slotcl", [128, NBLK], U32)
        eblast = sbuf("eblast", [128, 64], F32)
        hist_sb = sbuf("hist_sb", [1, E_EXP], F32)
        invT = sbuf("invT", [128, NSB], U32)
        scr_i = sbuf("scr_i", [128, 64], I32)
        scr_a = sbuf("scr_a", [128, 64], F32)
        scr_b = sbuf("scr_b", [128, 64], F32)
        stats6 = sbuf("stats6", [128, 24], F32)
        aggr2 = sbuf("aggr2", [128, 8], F32)
        zerot = sbuf("zerot", [128, 128], F32)

        WA = Arena(nc, "WA", 16384)
        TB = Arena(nc, "TB", 21760)

        psum = nc.alloc_psum_tensor("psA", [128, 4096], F32)
        PB = lambda i, n=1: psum[:, i * 512:(i + n) * 512]

        k.op("g", lambda e: e.memset(onesb[:, :], 1.0), writes=[onesb[:, :]])
        k.op("g", lambda e: e.memset(onesf[:, :], 1.0), writes=[onesf[:, :]])
        k.op("g", lambda e: e.memset(zerot[:, :], 0.0), writes=[zerot[:, :]])
        k.op("g", lambda e: e.affine_select(
            identb[:, :], onesb[:, :], [[-1, 128]], ALU.is_equal, 0.0,
            base=0, channel_multiplier=1),
            reads=[onesb[:, :]], writes=[identb[:, :]])
        k.op("g", lambda e: e.affine_select(
            identf[:, :], onesf[:, :], [[-1, 128]], ALU.is_equal, 0.0,
            base=0, channel_multiplier=1),
            reads=[onesf[:, :]], writes=[identf[:, :]])
        k.op("g", lambda e: e.affine_select(
            Lstrict[:, :], onesb[:, :], [[1, 128]], ALU.is_gt, 0.0,
            base=0, channel_multiplier=-1),
            reads=[onesb[:, :]], writes=[Lstrict[:, :]])
        k.op("g", lambda e: e.memset(scr_a[:, 0:64], -1.0 / GATE_NORM),
             writes=[scr_a[:, 0:64]])
        k.op("g", lambda e: e.memset(L64p[:, :, :], 0.0),
             writes=[L64p[:, :, :]])
        k.op("g", lambda e: e.memset(Lshp[:, :, :], 0.0),
             writes=[Lshp[:, :, :]])
        for hh in range(2):
            sl = slice(hh * 64, (hh + 1) * 64)
            k.op("g", lambda e, sl=sl, hh=hh: e.affine_select(
                L64p[sl, hh, :], scr_a[sl, 0:64], [[1, 64]], ALU.is_ge, 0.0,
                base=0, channel_multiplier=-1),
                reads=[scr_a[:, 0:64]], writes=[L64p[sl, hh, :]])
            k.op("g", lambda e, sl=sl, hh=hh: e.affine_select(
                Lshp[sl, hh, :], scr_a[sl, 0:64], [[-1, 64]], ALU.is_gt, 0.0,
                base=0, channel_multiplier=1),
                reads=[scr_a[:, 0:64]], writes=[Lshp[sl, hh, :]])
            k.op("g", lambda e, sl=sl: e.affine_select(
                maskT[sl, :], onesf[sl, 0:64], [[1, 64]], ALU.is_ge, 0.0,
                base=0, channel_multiplier=-1),
                reads=[onesf[:, :]], writes=[maskT[sl, :]])
        k.dma("q", coremask_sb[:, :], coremask_in[:, :], "dcc")
        k.dma("q", tokid_sb[:, :], tokid_in[:, :], "dcc")
        k.dma("q", myslots_sb[:, :], myslots_in[:, :], "dcc")
        k.dma("q", gnw_st[:, :], gnw_in[:, :], "dcc")
        k.dma("q", bgk2_st[:, :], bgk2_in[:, :], "dcc")
        for h in range(NH):
            k.op("t", lambda e, h=h: e.matmul(
                PB(0)[:, 0:HDV], onesf[0:1, 0:128], gnw_st[0:1, :],
                start=True, stop=True),
                reads=[onesf[:, :], gnw_st[:, :]], writes=[PB(0)[:, 0:HDV]],
                sig=("k", 1, 0))
            k.op("v", lambda e, h=h: e.tensor_copy(
                gnwb[:, h * HDV:(h + 1) * HDV], PB(0)[:, 0:HDV]),
                reads=[PB(0)[:, 0:HDV]], writes=[gnwb[:, :]])
        k.dma("g", wgk2x[0:16, :], wgk2_in[:, :], "dw0")
        bgk2_hl = sbuf("bgk2_hl", [1, 2 * DK], BF16)
        k.op("v", lambda e: e.tensor_copy(bgk2_hl[:, 0:DK], bgk2_st[:, :]),
             reads=[bgk2_st[:, :]], writes=[bgk2_hl[:, 0:DK]])
        k.op("v", lambda e: e.tensor_copy(bgk2_hf[:, :], bgk2_hl[:, 0:DK]),
             reads=[bgk2_hl[:, 0:DK]], writes=[bgk2_hf[:, :]])
        k.op("v", lambda e: e.tensor_sub(bgk2_hl[:, DK:2 * DK], bgk2_st[:, :],
                                         bgk2_hf[:, :]),
             reads=[bgk2_st[:, :], bgk2_hf[:, :]],
             writes=[bgk2_hl[:, DK:2 * DK]])
        k.dma("g", wgk2x[16:17, :], bgk2_hl[:, 0:DK], "dw0")
        k.dma("g", wgk2x[17:18, :], bgk2_hl[:, DK:2 * DK], "dw0")
        k.op("g", lambda e: e.memset(t1s[0:32, :], 1.0),
             writes=[t1s[0:32, :]])
        for kk in range(8):
            k.dma("q", wrf[:, kk, :], wr_in[kk * 128:(kk + 1) * 128, :], "dcc")
        for kk in range(8):
            k.op("t", lambda e, kk=kk: e.matmul(
                PB(1)[0:1, 0:E_EXP], onesf[:, 0:1], wrf[:, kk, :],
                start=(kk == 0), stop=(kk == 7)),
                reads=[onesf[:, :], wrf[:, kk, :]],
                writes=[PB(1)[0:1, 0:E_EXP]], inc=(kk == 7),
                sig=("k", 128, 0))
        k.op("v", lambda e: e.tensor_copy(colr[:, :], PB(1)[0:1, 0:E_EXP]),
             reads=[PB(1)[0:1, 0:E_EXP]], writes=[colr[:, :]])
        colrb = sbuf("colrb", [128, E_EXP], F32)
        k.op("t", lambda e: e.matmul(
            PB(2)[:, 0:E_EXP], onesf[0:1, 0:128], colr[:, :],
            start=True, stop=True),
            reads=[onesf[:, :], colr[:, :]], writes=[PB(2)[:, 0:E_EXP]],
            sig=("k", 1, 0))
        k.op("v", lambda e: e.tensor_copy(colrb[:, :], PB(2)[:, 0:E_EXP]),
             reads=[PB(2)[:, 0:E_EXP]], writes=[colrb[:, :]])

        wqs = WA.view(0, [128, 8, DK], BF16)
        wks = WA.view(4096, [128, 8, DK], BF16)
        wvs = WA.view(8192, [128, 8, DV], BF16)
        for kk in range(8):
            src = [(wqs, wq_in), (wks, wk_in)][0]
            k.dma("g", wqs[:, kk, :], wq_in[kk * 128:(kk + 1) * 128, :], "dw0")
            k.dma("g", wks[:, kk, :], wk_in[kk * 128:(kk + 1) * 128, :], "dw1")
            k.dma("g", wvs[:, kk, :], wv_in[kk * 128:(kk + 1) * 128, :], "dw0")
            k.dma("g", wgk1b[:, kk, :], wgk1_in[kk * 128:(kk + 1) * 128, :], "dw1")

        Sb = sbuf("Sb", [128, 2 * NH, HDV], BF16)
        scTe = sbuf("scTe", [128, NH, 64], BF16)
        scTo = sbuf("scTo", [128, NH, 64], BF16)
        oStage = TB.view(3072, [64, 2, DV], BF16)

        k.op("v", lambda e: e.memset(S[:, :, :], 0.0), writes=[S[:, :, :]])
        k.op("v", lambda e: e.memset(scTe[:, :, :], 0.0), writes=[scTe[:, :, :]])
        k.op("v", lambda e: e.memset(scTo[:, :, :], 0.0), writes=[scTo[:, :, :]])
        ln16c = sbuf("ln16c", [128, 1], F32)
        k.op("v", lambda e: e.memset(ln16c[:, :], -LN16), writes=[ln16c[:, :]])
        k.op("g", lambda e: e.memset(Sb[:, :, :], 0.0), writes=[Sb[:, :, :]])

        xsub = [TB.view(0, [128, D]), TB.view(1024, [128, D])]
        h1sub = [TB.view(2048, [128, D], BF16), TB.view(2560, [128, D], BF16)]
        h1fm = TB.view(3072, [128, 8, 512], BF16)
        qfm = TB.view(5120, [128, 8, 512], BF16)
        kfm = TB.view(7168, [128, 8, 512], BF16)
        ktm = TB.view(9216, [128, 4, D], BF16)
        vtm = TB.view(11264, [128, 4, DV], BF16)
        sp = TB.view(15360, [128, 4, D], F32)
        ebuf = TB.view(19456, [128, 512], BF16)
        ebsh_t = TB.view(19712, [128, D], BF16)
        kst2 = TB.view(20224, [128, 2, D], BF16)
        k.op("v", lambda e: e.memset(kst2[:, :, :], 0.0), writes=[kst2[:, :, :]])

        ptp = [psum[:, 2048:2112].bitcast(BF16),
               psum[:, 2560:2624].bitcast(BF16),
               psum[:, 2112:2176].bitcast(BF16),
               psum[:, 2624:2688].bitcast(BF16)]

        def ln_stats(xs, negmr_ap, r_ap, blkw=1024):
            nh = blkw // 512
            for i in range(nh):
                k.op("v", lambda e, i=i: e.bn_stats(
                    stats6[:, 6 * i:6 * i + 6], xs[:, 512 * i:512 * (i + 1)]),
                    reads=[xs], writes=[stats6[:, 6 * i:6 * i + 6]])
            k.op("v", lambda e: e.bn_aggr(aggr2[:, 0:2], stats6[:, 0:6 * nh]),
                 reads=[stats6[:, 0:6 * nh]], writes=[aggr2[:, 0:2]])
            k.op("v", lambda e: e.tensor_scalar_add(
                scr_a[:, 0:1], aggr2[:, 1:2], EPS),
                reads=[aggr2[:, 0:2]], writes=[scr_a[:, 0:1]])
            quake_rsqrt(k, r_ap, scr_a[:, 0:1], scr_i[:, 0:1], scr_b[:, 0:1])
            k.op("v", lambda e: e.tensor_scalar(
                negmr_ap, aggr2[:, 0:1], r_ap, -1.0, ALU.mult, ALU.mult),
                reads=[aggr2[:, 0:2], r_ap], writes=[negmr_ap])

        for ti in range(NT):
            trow = ti * 512
            for sub in range(NSUB):
                blk = ti * NSUB + sub
                xs = xsub[sub % 2]
                k.dma("q", xs, x_in[trow + sub * 128: trow + (sub + 1) * 128, :],
                      "dx%d" % (sub % 2))
                nmr = m1r1[:, 2 * blk:2 * blk + 1]
                rr = m1r1[:, 2 * blk + 1:2 * blk + 2]
                ln_stats(xs, nmr, rr)
                hs = h1sub[sub % 2]
                k.op("s", lambda e, xs=xs, hs=hs, nmr=nmr, rr=rr: e.activation(
                    hs, xs, AF.Identity, bias=nmr, scale=rr),
                    reads=[xs, nmr, rr], writes=[hs])
                for f in range(8):
                    pt = ptp[f % 4]
                    k.op("t", lambda e, hs=hs, f=f, pt=pt: e.transpose(
                        pt, hs[:, f * 128:(f + 1) * 128], identb[:, :]),
                        reads=[hs, identb[:, :]], writes=[pt],
                        sig=("k", 128, 0))
                    dst = h1fm[:, f, sub * 128:(sub + 1) * 128]
                    eng = "s"
                    k.op(eng, lambda e, dst=dst, pt=pt: (
                        e.tensor_copy(dst, pt) if hasattr(e, "tensor_copy")
                        else e.copy(dst, pt)),
                        reads=[pt], writes=[dst])

            if pa >= 2:
                for m in range(8):
                    pq = PB(m % 4)
                    for kk in range(8):
                        k.op("t", lambda e, m=m, kk=kk, pq=pq: e.matmul(
                            pq, wqs[:, kk, m * 128:(m + 1) * 128], h1fm[:, kk, :],
                            start=(kk == 0), stop=(kk == 7)),
                            reads=[wqs[:, kk, :], h1fm[:, kk, :]], writes=[pq],
                            inc=(kk == 7), sig=("k", 128, 0))
                    eng = "v" if m % 2 else "s"
                    k.op(eng, lambda e, m=m, pq=pq: (
                        e.tensor_copy(qfm[:, m, :], pq) if hasattr(e, "tensor_copy")
                        else e.copy(qfm[:, m, :], pq)),
                        reads=[pq], writes=[qfm[:, m, :]])
                for m in range(8):
                    pq = PB(m % 4)
                    for kk in range(8):
                        k.op("t", lambda e, m=m, kk=kk, pq=pq: e.matmul(
                            pq, wks[:, kk, m * 128:(m + 1) * 128], h1fm[:, kk, :],
                            start=(kk == 0), stop=(kk == 7)),
                            reads=[wks[:, kk, :], h1fm[:, kk, :]], writes=[pq],
                            inc=(kk == 7), sig=("k", 128, 0))
                    eng = "v" if m % 2 else "s"
                    k.op(eng, lambda e, m=m, pq=pq: (
                        e.tensor_copy(kfm[:, m, :], pq) if hasattr(e, "tensor_copy")
                        else e.copy(kfm[:, m, :], pq)),
                        reads=[pq], writes=[kfm[:, m, :]])
                for sub in range(NSUB):
                    pq = PB(4 if sub % 2 == 0 else 6, 2)
                    for nh in range(2):
                        for kk in range(8):
                            k.op("t", lambda e, sub=sub, kk=kk, nh=nh, pq=pq:
                                 e.matmul(
                                     pq[:, nh * 512:(nh + 1) * 512],
                                     h1fm[:, kk, sub * 128:(sub + 1) * 128],
                                     wks[:, kk, nh * 512:(nh + 1) * 512],
                                     start=(kk == 0), stop=(kk == 7)),
                                 reads=[wks[:, kk, :], h1fm[:, kk, :]],
                                 writes=[pq[:, nh * 512:(nh + 1) * 512]],
                                 inc=(kk == 7), sig=("k", 128, 0))
                    eng = "v" if sub % 2 else "s"
                    k.op(eng, lambda e, sub=sub, pq=pq: (
                        e.tensor_copy(ktm[:, sub, :], pq) if hasattr(e, "tensor_copy")
                        else e.copy(ktm[:, sub, :], pq)),
                        reads=[pq], writes=[ktm[:, sub, :]])
                for sub in range(NSUB):
                    for half in range(2):
                        pq = PB(4 if (2 * sub + half) % 2 == 0 else 6, 2)
                        for nh in range(2):
                            for kk in range(8):
                                k.op("t", lambda e, sub=sub, kk=kk, half=half,
                                     nh=nh, pq=pq: e.matmul(
                                         pq[:, nh * 512:(nh + 1) * 512],
                                         h1fm[:, kk, sub * 128:(sub + 1) * 128],
                                         wvs[:, kk,
                                             half * 1024 + nh * 512:
                                             half * 1024 + (nh + 1) * 512],
                                         start=(kk == 0), stop=(kk == 7)),
                                     reads=[wvs[:, kk, :], h1fm[:, kk, :]],
                                     writes=[pq[:, nh * 512:(nh + 1) * 512]],
                                     inc=(kk == 7), sig=("k", 128, 0))
                        dst = vtm[:, sub, half * 1024:(half + 1) * 1024]
                        eng = "v" if half else "s"
                        k.op(eng, lambda e, dst=dst, pq=pq: (
                            e.tensor_copy(dst, pq) if hasattr(e, "tensor_copy")
                            else e.copy(dst, pq)),
                            reads=[pq], writes=[dst])
                pt1 = PB(0)
                for kk in range(8):
                    k.op("t", lambda e, kk=kk: e.matmul(
                        pt1[0:16, :], wgk1b[:, kk, :], h1fm[:, kk, :],
                        start=(kk == 0), stop=(kk == 7)),
                        reads=[wgk1b[:, kk, :], h1fm[:, kk, :]],
                        writes=[pt1[0:16, :]], inc=(kk == 7), sig=("k", 128, 0))
                k.op("v", lambda e: e.tensor_copy(t1s[0:16, :], pt1[0:16, :]),
                     reads=[pt1[0:16, :]], writes=[t1s[0:16, :]])
                for sub in range(NSUB):
                    pq = PB(1 if sub % 2 == 0 else 2, 2)
                    pq = PB(0, 2) if sub % 2 == 0 else PB(2, 2)
                    for nh in range(2):
                        k.op("t", lambda e, sub=sub, nh=nh, pq=pq: e.matmul(
                            pq[:, nh * 512:(nh + 1) * 512],
                            t1s[0:18, sub * 128:(sub + 1) * 128],
                            wgk2x[0:18, nh * 512:(nh + 1) * 512],
                            start=True, stop=True),
                            reads=[t1s[0:18, :], wgk2x[0:18, :]],
                            writes=[pq[:, nh * 512:(nh + 1) * 512]],
                            inc=(nh == 1), sig=("k", 18, 0))
                    spv = sp[:, sub, :]
                    k.op("s", lambda e, spv=spv, pq=pq: e.activation(
                        spv, pq, AF.Exp, scale=-1.0),
                        reads=[pq], writes=[spv])
                    k.op("v", lambda e, spv=spv: e.tensor_scalar_add(spv, spv, 1.0),
                         reads=[spv], writes=[spv])
                    k.op("s", lambda e, spv=spv: e.activation(spv, spv, AF.Ln),
                         reads=[spv], writes=[spv])

            if pa >= 3:
                for ht in range(8):
                    pq = PB(4) if ht % 2 == 0 else PB(5)
                    for c in range(8):
                        k.op("t", lambda e, ht=ht, c=c, pq=pq: e.matmul(
                            pq[:, c * 64:(c + 1) * 64],
                            sp[:, c // 2, ht * 128:(ht + 1) * 128],
                            L64p[:, c % 2, :],
                            start=True, stop=True),
                            reads=[sp[:, c // 2, :], L64p[:, :, :]],
                            writes=[pq[:, c * 64:(c + 1) * 64]],
                            inc=(c == 7), sig=("k", 128, 0))
                    pr = pq.rearrange("p (c e) -> p c e", c=8)
                    k.op("s", lambda e, ht=ht, pr=pr: e.activation(
                        eblast[:, 8 * ht:8 * ht + 8], pr[:, :, 63:64], AF.Exp),
                        reads=[pr[:, :, 63:64]], writes=[eblast[:, 8 * ht:8 * ht + 8]])
                    k.op("s", lambda e, pq=pq: e.activation(
                        ebuf, pq, AF.Exp, bias=ln16c[:, :]),
                        reads=[pq, ln16c[:, :]], writes=[ebuf])
                    k.op("v", lambda e, ht=ht: e.tensor_mul(
                        qfm[:, ht, :], qfm[:, ht, :], ebuf),
                        reads=[qfm[:, ht, :], ebuf], writes=[qfm[:, ht, :]])
                    k.op("s", lambda e, pq=pq: e.activation(
                        ebuf, pq, AF.Exp, scale=-1.0),
                        reads=[pq], writes=[ebuf])
                    k.op("v", lambda e, ht=ht: e.tensor_mul(
                        kfm[:, ht, :], kfm[:, ht, :], ebuf),
                        reads=[kfm[:, ht, :], ebuf], writes=[kfm[:, ht, :]])

            if pa >= 4:
                ov_all = psum[0:64, 1536:3584]
                for c in range(8):
                    sub, r0 = c // 2, (c % 2) * 64
                    par = c % 2
                    cs = slice(c * 64, (c + 1) * 64)
                    scp = (scTe, scTo)[par]
                    if par == 0:
                        for nh in range(2):
                            pb7 = PB(7)
                            for p2 in range(2):
                                k.op("t", lambda e, sub=sub, nh=nh, p2=p2, pb7=pb7: e.matmul(
                                    pb7[p2 * 64:(p2 + 1) * 64, :],
                                    Lshp[:, p2, :],
                                    sp[:, sub, nh * 512:(nh + 1) * 512],
                                    start=True, stop=True, tile_position=(0, p2 * 64)),
                                    reads=[Lshp[:, :, :], sp[:, sub, :]],
                                    writes=[pb7[p2 * 64:(p2 + 1) * 64, :]],
                                    inc=(p2 == 1), sig=("k", 128, 0))
                            k.op("s", lambda e, nh=nh, pb7=pb7: e.activation(
                                ebsh_t[:, nh * 512:(nh + 1) * 512], pb7, AF.Exp),
                                reads=[pb7], writes=[ebsh_t[:, nh * 512:(nh + 1) * 512]])
                        for p2 in range(2):
                            q0 = p2 * 64
                            k.op("v", lambda e, sub=sub, p2=p2, q0=q0: e.tensor_mul(
                                kst2[q0:q0 + 64, p2, :], ktm[q0:q0 + 64, sub, :],
                                ebsh_t[q0:q0 + 64, :]),
                                reads=[ktm[:, sub, :], ebsh_t[:, :]],
                                writes=[kst2[q0:q0 + 64, p2, :]])
                    for h in range(NH):
                        sc = PB(2)[r0:r0 + 64, h * 64:(h + 1) * 64]
                        for j in range(2):
                            ht = 2 * h + j
                            k.op("t", lambda e, ht=ht, cs=cs, sc=sc, r0=r0, j=j:
                                 e.matmul(
                                     sc, kfm[:, ht, cs], qfm[:, ht, cs],
                                     start=(j == 0), stop=(j == 1),
                                     tile_position=(0, r0)),
                                 reads=[kfm[:, ht, cs], qfm[:, ht, cs]],
                                 writes=[sc], inc=(j == 1),
                                 sig=("k", 128, 0))
                    for h in range(NH):
                        sc = PB(2)[r0:r0 + 64, h * 64:(h + 1) * 64]
                        k.op("v", lambda e, h=h, sc=sc, r0=r0, scp=scp: e.tensor_mul(
                            scp[r0:r0 + 64, h, :], sc, maskT[r0:r0 + 64, :]),
                            reads=[sc, maskT[:, :]], writes=[scp[r0:r0 + 64, h, :]])
                    for h in range(NH):
                        ov = ov_all[:, h * 512:(h + 1) * 512]
                        vsl = vtm[:, sub, h * 512:(h + 1) * 512]
                        k.op("t", lambda e, h=h, ov=ov, vsl=vsl, scp=scp: e.matmul(
                            ov, scp[:, h, :], vsl, start=True, stop=False),
                            reads=[scp[:, h, :], vsl], writes=[ov],
                            inc=False, sig=("k", 128, 0))
                    for h in range(NH):
                        ov = ov_all[:, h * 512:(h + 1) * 512]
                        for j in range(2):
                            ht = 2 * h + j
                            k.op("t", lambda e, ht=ht, cs=cs, ov=ov, j=j: e.matmul(
                                ov, qfm[:, ht, cs], Sb[:, ht, :],
                                start=False, stop=(j == 1)),
                                reads=[qfm[:, ht, cs], Sb[:, ht, :]],
                                writes=[ov], inc=(j == 1),
                                sig=("k", 128, 0))
                    for h in range(NH):
                        for j in range(2):
                            ht = 2 * h + j
                            pd = PB((2 * h + j) % 3)
                            ksl = kst2[:, par, h * 256 + j * 128: h * 256 + (j + 1) * 128]
                            vsl2 = vtm[:, sub, h * 512:(h + 1) * 512]
                            k.op("t", lambda e, pd=pd, ksl=ksl, vsl2=vsl2:
                                 e.matmul(pd, ksl, vsl2, start=True, stop=True),
                                 reads=[ksl, vsl2], writes=[pd],
                                 sig=("k", 128, 0))
                            k.op("v", lambda e, ht=ht, c=c, pd=pd:
                                 e.scalar_tensor_tensor(
                                     S[:, ht, :], S[:, ht, :],
                                     eblast[:, 8 * ht + c:8 * ht + c + 1], pd,
                                     ALU.mult, ALU.add),
                                 reads=[S[:, ht, :],
                                        eblast[:, 8 * ht + c:8 * ht + c + 1], pd],
                                 writes=[S[:, ht, :]])
                            k.op("s", lambda e, ht=ht: e.copy(
                                Sb[:, ht, :], S[:, ht, :]),
                                reads=[S[:, ht, :]], writes=[Sb[:, ht, :]])
                    ost = oStage[0:64, c % 2, :]
                    eng = "s" if c % 2 == 0 else "v"
                    k.op(eng, lambda e, ost=ost: (
                        e.copy(ost, ov_all) if hasattr(e, "copy")
                        else e.tensor_copy(ost, ov_all)),
                        reads=[ov_all], writes=[ost])
                    k.dma("g", od[trow + c * 64: trow + (c + 1) * 64, :], ost,
                          "dos%d" % (c % 2))

        if phase_limit >= 2:
            wgB = WA.view(0, [128, 8, DV], BF16)
            woB = WA.view(8192, [128, 16, D], BF16)
            for kk in range(8):
                k.dma("g", wgB[:, kk, :], wg_in[kk * 128:(kk + 1) * 128, :], "dw0")
            for kk in range(16):
                k.dma("g", woB[:, kk, :], wo_in[kk * 128:(kk + 1) * 128, :], "dw1")

            h1fmB = TB.view(3072, [128, 8, 512], BF16)
            opair = [TB.view(5120, [128, DV], BF16), TB.view(6144, [128, DV], BF16)]
            gsil = TB.view(7168, [128, DV], BF16)
            ofsub = [TB.view(8192, [128, 16, 128], BF16),
                     TB.view(9216, [128, 16, 128], BF16)]
            x1sub = [TB.view(10240, [128, D]), TB.view(11264, [128, D])]
            x1fm = [TB.view(12288, [128, 8, 128]), TB.view(13312, [128, 8, 128])]
            h2sub = [TB.view(14336, [128, D], BF16), TB.view(14848, [128, D], BF16)]
            ofin = TB.view(15872, [128, DV], BF16)
            tbuf = TB.view(16384, [128, D])
            iotaE = sbuf("iotaE", [128, E_EXP], F32)
            zsb = sbuf("zsb", [128, 128], F32)
            nmrow = sbuf("nmrow", [1, 128], F32)
            ztm = sbuf("ztm", [128, E_EXP], F32)
            mx8 = sbuf("mx8", [128, 8], F32)
            ngmx = sbuf("ngmx", [128, 1], F32)
            zt2 = sbuf("zt2", [128, E_EXP], F32)
            thb = sbuf("thb", [128, E_EXP], F32)
            numb = sbuf("numb", [128, E_EXP], F32)
            denb = sbuf("denb", [128, E_EXP], F32)
            fracb = sbuf("fracb", [128, E_EXP], F32)
            selb = sbuf("selb", [128, E_EXP], F32)
            posf = sbuf("posf", [128, E_EXP], F32)
            ssum1 = sbuf("ssum1", [128, 1], F32)
            rms4 = sbuf("rms4", [128, 8], F32)

            k.op("g", lambda e: e.iota(scr_i[:, 0:E_EXP], [[1, E_EXP]],
                                       base=0, channel_multiplier=0),
                 writes=[scr_i[:, 0:E_EXP]])
            k.op("v", lambda e: e.tensor_copy(iotaE[:, :], scr_i[:, 0:E_EXP]),
                 reads=[scr_i[:, 0:E_EXP]], writes=[iotaE[:, :]])

            ptf = [psum[:, 2048:2176], psum[:, 2560:2688],
                   psum[:, 2176:2304], psum[:, 2688:2816]]
            pbase = PB(7)[:, 0:E_EXP]
            basef = sbuf("basef", [128, E_EXP], F32)
            k.op("v", lambda e: e.memset(basef[:, :], 0.0), writes=[basef[:, :]])

            NS = NT * NSUB
            xsubB = [TB.view(0, [128, D]), TB.view(1024, [128, D]),
                     TB.view(17408, [128, D]), TB.view(18432, [128, D])]

            def emitP(s):
                ti, sub = divmod(s, NSUB)
                trow = ti * 512
                xs = xsubB[s % 4]
                k.dma("q", xs, x_in[trow + sub * 128: trow + (sub + 1) * 128, :],
                      "dx%d" % (s % 2))
                ot = opair[s % 2]
                k.dma("q", ot, od[trow + sub * 128: trow + (sub + 1) * 128, :],
                      "dor%d" % (s % 2))

            def emitH1(s):
                ti, sub = divmod(s, NSUB)
                blk = s
                pp = s % 2
                xs = xsubB[s % 4]
                hs = h1sub[pp]
                nmr = m1r1[:, 2 * blk:2 * blk + 1]
                rr = m1r1[:, 2 * blk + 1:2 * blk + 2]
                k.op("s", lambda e, xs=xs, hs=hs, nmr=nmr, rr=rr: e.activation(
                    hs, xs, AF.Identity, bias=nmr, scale=rr),
                    reads=[xs, nmr, rr], writes=[hs])
                for f in range(8):
                    pt = ptp[f % 4]
                    k.op("t", lambda e, hs=hs, f=f, pt=pt: e.transpose(
                        pt, hs[:, f * 128:(f + 1) * 128], identb[:, :]),
                        reads=[hs, identb[:, :]], writes=[pt], sig=("k", 128, 0))
                    dst = h1fmB[:, f, sub * 128:(sub + 1) * 128]
                    eng = "s"
                    k.op(eng, lambda e, dst=dst, pt=pt: (
                        e.tensor_copy(dst, pt) if hasattr(e, "tensor_copy")
                        else e.copy(dst, pt)),
                        reads=[pt], writes=[dst])

            def emitS1a(s):
                ti, sub = divmod(s, NSUB)
                blk = s
                pp = s % 2
                pg = psum[:, 0:2048]
                for half in range(4):
                    for kk in range(8):
                        k.op("t", lambda e, sub=sub, kk=kk, half=half, pg=pg:
                             e.matmul(
                                 pg[:, half * 512:(half + 1) * 512],
                                 h1fmB[:, kk, sub * 128:(sub + 1) * 128],
                                 wgB[:, kk, half * 512:(half + 1) * 512],
                                 start=(kk == 0), stop=(kk == 7)),
                             reads=[h1fmB[:, kk, :], wgB[:, kk, :]],
                             writes=[pg[:, half * 512:(half + 1) * 512]],
                             inc=(kk == 7), sig=("k", 128, 0))
                k.op("s", lambda e, pg=pg: e.activation(gsil, pg, AF.Silu),
                     reads=[pg], writes=[gsil])
                k.op("v", lambda e: e.tensor_mul(gsil, gsil, gnwb[:, :]),
                     reads=[gsil, gnwb[:, :]], writes=[gsil])

            def emitRMS(s):
                pp = s % 2
                ot = opair[pp]
                r4 = rms4[:, 4 * pp:4 * pp + 4]
                for h in range(NH):
                    k.op("v", lambda e, h=h, ot=ot: e.bn_stats(
                        stats6[:, 6 * h:6 * h + 6],
                        ot[:, h * HDV:(h + 1) * HDV]),
                        reads=[ot], writes=[stats6[:, 6 * h:6 * h + 6]])
                    k.op("v", lambda e, h=h: e.bn_aggr(
                        aggr2[:, 2 * h:2 * h + 2], stats6[:, 6 * h:6 * h + 6]),
                        reads=[stats6[:, 6 * h:6 * h + 6]],
                        writes=[aggr2[:, 2 * h:2 * h + 2]])
                    k.op("v", lambda e, h=h: e.tensor_scalar(
                        scr_a[:, h:h + 1], aggr2[:, 2 * h:2 * h + 1],
                        aggr2[:, 2 * h:2 * h + 1], None, ALU.mult),
                        reads=[aggr2[:, 2 * h:2 * h + 2]],
                        writes=[scr_a[:, h:h + 1]])
                    k.op("v", lambda e, h=h: e.tensor_add(
                        scr_a[:, h:h + 1], scr_a[:, h:h + 1],
                        aggr2[:, 2 * h + 1:2 * h + 2]),
                        reads=[scr_a[:, h:h + 1], aggr2[:, 2 * h:2 * h + 2]],
                        writes=[scr_a[:, h:h + 1]])
                k.op("v", lambda e: e.tensor_scalar_add(
                    scr_a[:, 0:4], scr_a[:, 0:4], EPS),
                    reads=[scr_a[:, 0:4]], writes=[scr_a[:, 0:4]])
                quake_rsqrt(k, r4, scr_a[:, 0:4], scr_i[:, 0:4],
                            scr_b[:, 0:4])

            def emitS1b(s):
                ti, sub = divmod(s, NSUB)
                blk = s
                pp = s % 2
                xs = xsubB[s % 4]
                trow = ti * 512
                ot = opair[pp]
                r4 = rms4[:, 4 * pp:4 * pp + 4]
                k.op("v", lambda e, ot=ot: e.tensor_mul(ofin, ot, gsil),
                     reads=[ot, gsil], writes=[ofin])
                for h in range(NH):
                    eng = "s" if h % 2 else "v"
                    sl = ofin[:, h * HDV:(h + 1) * HDV]
                    if eng == "v":
                        k.op("v", lambda e, sl=sl, h=h, r4=r4: e.tensor_scalar_mul(
                            sl, sl, r4[:, h:h + 1]),
                            reads=[sl, r4], writes=[sl])
                    else:
                        k.op("s", lambda e, sl=sl, h=h, r4=r4: e.activation(
                            sl, sl, AF.Identity, scale=r4[:, h:h + 1]),
                            reads=[sl, r4], writes=[sl])
                for f in range(16):
                    pt = ptp[f % 4]
                    k.op("t", lambda e, f=f, pt=pt: e.transpose(
                        pt, ofin[:, f * 128:(f + 1) * 128], identb[:, :]),
                        reads=[ofin, identb[:, :]], writes=[pt], sig=("k", 128, 0))
                    dst = ofsub[pp][:, f, :]
                    eng = "v" if f % 2 else "s"
                    k.op(eng, lambda e, dst=dst, pt=pt: (
                        e.tensor_copy(dst, pt) if hasattr(e, "tensor_copy")
                        else e.copy(dst, pt)),
                        reads=[pt], writes=[dst])
                pw = psum[:, 0:1024]
                for nh in range(2):
                    for kk in range(16):
                        k.op("t", lambda e, kk=kk, pp=pp, nh=nh, pw=pw:
                             e.matmul(
                                 pw[:, nh * 512:(nh + 1) * 512],
                                 ofsub[pp][:, kk, :],
                                 woB[:, kk, nh * 512:(nh + 1) * 512],
                                 start=(kk == 0), stop=(kk == 15)),
                             reads=[ofsub[pp][:, kk, :], woB[:, kk, :]],
                             writes=[pw[:, nh * 512:(nh + 1) * 512]],
                             inc=(kk == 15), sig=("k", 128, 0))
                for i in range(2):
                    k.op("v", lambda e, i=i, pw=pw: e.bn_stats(
                        stats6[:, 6 * i:6 * i + 6],
                        pw[:, 512 * i:512 * (i + 1)]),
                        reads=[pw], writes=[stats6[:, 6 * i:6 * i + 6]])
                k.op("v", lambda e: e.bn_aggr(aggr2[:, 0:2], stats6[:, 0:12]),
                     reads=[stats6[:, 0:12]], writes=[aggr2[:, 0:2]])
                k.op("v", lambda e: e.tensor_scalar_add(
                    scr_a[:, 0:1], aggr2[:, 1:2], EPS),
                    reads=[aggr2[:, 0:2]], writes=[scr_a[:, 0:1]])
                quake_rsqrt(k, scr_b[:, 1:2], scr_a[:, 0:1], scr_i[:, 0:1],
                            scr_b[:, 0:1])
                k.op("v", lambda e: e.tensor_scalar(
                    scr_a[:, 1:2], aggr2[:, 0:1], scr_b[:, 1:2], -1.0,
                    ALU.mult, ALU.mult),
                    reads=[aggr2[:, 0:2], scr_b[:, 1:2]],
                    writes=[scr_a[:, 1:2]])
                k.op("s", lambda e, pw=pw: e.activation(
                    tbuf, pw, AF.Identity, bias=scr_a[:, 1:2],
                    scale=scr_b[:, 1:2]),
                    reads=[pw, scr_a[:, 1:2], scr_b[:, 1:2]], writes=[tbuf])
                x1s = x1sub[pp]
                k.op("v", lambda e, x1s=x1s, xs=xs: e.tensor_add(
                    x1s, xs, tbuf),
                    reads=[xs, tbuf], writes=[x1s])
                k.dma("g", x1d[trow + sub * 128: trow + (sub + 1) * 128, :],
                      x1s, "dx1s")

            def emitS1c(s):
                ti, sub = divmod(s, NSUB)
                blk = s
                pp = s % 2
                trow = ti * 512
                x1s = x1sub[pp]
                ln_stats(x1s, m2r2[:, 2 * blk:2 * blk + 1], m2r2[:, 2 * blk + 1:2 * blk + 2])
                k.op("v", lambda e, blk=blk: e.tensor_scalar_mul(
                    nm2s[:, blk:blk + 1], aggr2[:, 0:1], -1.0),
                    reads=[aggr2[:, 0:2]], writes=[nm2s[:, blk:blk + 1]])
                h2s = h2sub[pp]
                k.op("s", lambda e, x1s=x1s, h2s=h2s, blk=blk: e.activation(
                    h2s, x1s, AF.Identity, bias=m2r2[:, 2 * blk:2 * blk + 1],
                    scale=m2r2[:, 2 * blk + 1:2 * blk + 2]),
                    reads=[x1s, m2r2[:, 2 * blk:2 * blk + 1], m2r2[:, 2 * blk + 1:2 * blk + 2]],
                    writes=[h2s])
                k.dma("g", h2d[trow + sub * 128: trow + (sub + 1) * 128, :],
                      h2s, "dh2s")
                if sub == NSUB - 1 and ti < NT - 1:
                    emit_h2ag(ti)

            def emitS2(s):
                blk = s
                pp = s % 2
                x1s = x1sub[pp]
                for f in range(8):
                    pt = ptf[f % 4]
                    k.op("t", lambda e, x1s=x1s, f=f, pt=pt: e.transpose(
                        pt, x1s[:, f * 128:(f + 1) * 128], identf[:, :]),
                        reads=[x1s, identf[:, :]], writes=[pt], sig=("k", 128, 0))
                    dst = x1fm[pp][:, f, :]
                    eng = "s"
                    k.op(eng, lambda e, dst=dst, pt=pt: (
                        e.tensor_copy(dst, pt) if hasattr(e, "tensor_copy")
                        else e.copy(dst, pt)),
                        reads=[pt], writes=[dst])
                pz = PB(6)[0:32, 0:128]
                for kk in range(8):
                    k.op("t", lambda e, kk=kk, pp=pp, pz=pz: e.matmul(
                        pz, wrf[:, kk, :], x1fm[pp][:, kk, :],
                        start=(kk == 0), stop=(kk == 7)),
                        reads=[wrf[:, kk, :], x1fm[pp][:, kk, :]],
                        writes=[pz], inc=(kk == 7), sig=("k", 128, 0))
                k.op("s", lambda e, pz=pz: e.copy(zsb[0:32, :], pz),
                     reads=[pz], writes=[zsb[:, :]])
                k.op("t", lambda e: e.transpose(
                    ptf[1], zsb[:, :], identf[:, :]),
                    reads=[zsb[:, :], identf[:, :]], writes=[ptf[1]], sig=("k", 128, 0))
                k.op("s", lambda e: e.copy(ztm[:, :], ptf[1][:, 0:32]),
                     reads=[ptf[1]], writes=[ztm[:, :]])
                k.op("v", lambda e, blk=blk: e.scalar_tensor_tensor(
                    ztm[:, :], colrb[:, :], nm2s[:, blk:blk + 1], ztm[:, :],
                    ALU.mult, ALU.add),
                    reads=[colrb[:, :], nm2s[:, blk:blk + 1], ztm[:, :]],
                    writes=[ztm[:, :]])
                k.op("v", lambda e: e.max(mx8[:, :], ztm[:, :]),
                     reads=[ztm[:, :]], writes=[mx8[:, :]])
                k.op("v", lambda e, blk=blk: e.tensor_scalar(
                    onehot[:, blk * E_EXP:(blk + 1) * E_EXP], ztm[:, :], mx8[:, 0:1], None,
                    ALU.is_equal),
                    reads=[ztm[:, :], mx8[:, :]], writes=[onehot[:, blk * E_EXP:(blk + 1) * E_EXP]])
                k.op("v", lambda e, blk=blk: e.tensor_mul(
                    selb[:, :], onehot[:, blk * E_EXP:(blk + 1) * E_EXP], iotaE[:, :]),
                    reads=[onehot[:, blk * E_EXP:(blk + 1) * E_EXP], iotaE[:, :]], writes=[selb[:, :]])
                k.op("v", lambda e, blk=blk: e.reduce_sum(
                    top1[:, blk:blk + 1], selb[:, :], AX.X),
                    reads=[selb[:, :]], writes=[top1[:, blk:blk + 1]])
                k.op("v", lambda e: e.tensor_scalar_mul(ngmx[:, :], mx8[:, 0:1],
                                                        -1.0),
                     reads=[mx8[:, :]], writes=[ngmx[:, :]])
                k.op("v", lambda e, blk=blk: e.tensor_scalar(
                    zt2[:, :], ztm[:, :], ngmx[:, 0:1], m2r2[:, 2 * blk + 1:2 * blk + 2],
                    ALU.add, ALU.mult),
                    reads=[ztm[:, :], ngmx[:, :], m2r2[:, 2 * blk + 1:2 * blk + 2]],
                    writes=[zt2[:, :]])
                k.op("s", lambda e: e.activation(thb[:, :], zt2[:, :], AF.Tanh,
                                                 scale=0.5),
                     reads=[zt2[:, :]], writes=[thb[:, :]])
                k.op("v", lambda e: e.tensor_scalar_add(numb[:, :], thb[:, :],
                                                        1.0),
                     reads=[thb[:, :]], writes=[numb[:, :]])
                k.op("v", lambda e: e.tensor_scalar(
                    denb[:, :], thb[:, :], -1.0, 1.0, ALU.mult, ALU.add),
                    reads=[thb[:, :]], writes=[denb[:, :]])
                k.op("v", lambda e: e.reciprocal(denb[:, :], denb[:, :]),
                     reads=[denb[:, :]], writes=[denb[:, :]])
                k.op("v", lambda e: e.tensor_mul(
                    fracb[:, :], numb[:, :], denb[:, :]),
                    reads=[numb[:, :], denb[:, :]], writes=[fracb[:, :]])
                k.op("v", lambda e: e.reduce_sum(ssum1[:, :], fracb[:, :], AX.X),
                     reads=[fracb[:, :]], writes=[ssum1[:, :]])
                k.op("v", lambda e, blk=blk: e.reciprocal(
                    gatem[:, blk:blk + 1], ssum1[:, :]),
                    reads=[ssum1[:, :]], writes=[gatem[:, blk:blk + 1]])
                pe = PB(6)[:, 0:E_EXP]
                oh = onehot[:, blk * E_EXP:(blk + 1) * E_EXP]
                k.op("t", lambda e, pe=pe, oh=oh: e.matmul(
                    pe, Lstrict[:, :], oh, start=True, stop=True),
                    reads=[Lstrict[:, :], oh], writes=[pe], sig=("k", 128, 0))
                k.op("t", lambda e, oh=oh: e.matmul(
                    pbase, onesb[:, :], oh, start=True, stop=True),
                    reads=[onesb[:, :], oh], writes=[pbase], sig=("k", 128, 0))
                k.op("v", lambda e, pe=pe: e.tensor_add(
                    posf[:, :], basef[:, :], pe),
                    reads=[basef[:, :], pe], writes=[posf[:, :]])
                k.op("v", lambda e, oh=oh: e.tensor_mul(
                    selb[:, :], posf[:, :], oh),
                    reads=[posf[:, :], oh], writes=[selb[:, :]])
                k.op("v", lambda e, blk=blk: e.reduce_sum(
                    poss[:, blk:blk + 1], selb[:, :], AX.X),
                    reads=[selb[:, :]], writes=[poss[:, blk:blk + 1]])
                k.op("v", lambda e: e.tensor_add(basef[:, :], basef[:, :],
                                                 pbase),
                     reads=[basef[:, :], pbase], writes=[basef[:, :]])

            def emit_h2ag(ti):
                k.op("g", lambda e, ti=ti: e.collective_compute(
                    "AllGather", ALU.bypass, RG,
                    [h2d[ti * 512:(ti + 1) * 512, :]],
                    [h2ag[ti * n_cores * 512:(ti + 1) * n_cores * 512, :]]),
                    reads=[h2d[ti * 512:(ti + 1) * 512, :]],
                    writes=[h2ag[ti * n_cores * 512:
                                 (ti + 1) * n_cores * 512, :]],
                    chain="hcc%d" % ti, serial=True)

            emitP(0)
            emitP(1)
            emitH1(0)
            emitRMS(0)
            for s in range(NS):
                emitS1a(s)
                if s + 1 < NS:
                    emitH1(s + 1)
                emitS1b(s)
                if s + 1 < NS:
                    emitRMS(s + 1)
                if s >= 1:
                    emitS2(s - 1)
                emitS1c(s)
                if s + 2 < NS:
                    emitP(s + 2)
            emitS2(NS - 1)
            k.op("v", lambda e: e.tensor_copy(hist_sb[:, :], basef[0:1, :]),
                 reads=[basef[:, :]], writes=[hist_sb[:, :]])

        if phase_limit >= 3:
            trashc = sbuf("trashc", [128, 1], F32)
            goffrow = sbuf("goffrow", [1, E_EXP], F32)
            k.op("g", lambda e: e.iota(scr_i[:, 4:5], [[0, 1]],
                                       base=GRID, channel_multiplier=1),
                 writes=[scr_i[:, 4:5]])
            k.op("v", lambda e: e.tensor_copy(trashc[:, :], scr_i[:, 4:5]),
                 reads=[scr_i[:, 4:5]], writes=[trashc[:, :]])

            k.dma("g", cin_h[:, :], hist_sb[:, :], "dci")
            k.op("g", lambda e: e.collective_compute(
                "AllGather", ALU.bypass, RG, [cin_h[:, :]], [cout_h[:, :]]),
                reads=[cin_h[:, :]], writes=[cout_h[:, :]], chain="icc0", serial=True)
            emit_h2ag(NT - 1)
            k.dma("q", histag[:, :], cout_h[:, :], "dcc")
            k.op("t", lambda e: e.matmul(
                PB(6)[0:1, 0:E_EXP], coremask_sb[0:n_cores, 0:1],
                histag[0:n_cores, :], start=True, stop=True),
                reads=[coremask_sb[:, :], histag[:, :]],
                writes=[PB(6)[0:1, 0:E_EXP]], sig=("k", n_cores, 0))
            k.op("v", lambda e: e.tensor_copy(goffrow[:, :], PB(6)[0:1, 0:E_EXP]),
                 reads=[PB(6)[0:1, 0:E_EXP]], writes=[goffrow[:, :]])
            k.op("t", lambda e: e.matmul(
                PB(6)[:, 0:E_EXP], onesf[0:1, 0:128], goffrow[:, :],
                start=True, stop=True),
                reads=[onesf[:, :], goffrow[:, :]], writes=[PB(6)[:, 0:E_EXP]], sig=("k", 1, 0))
            k.op("v", lambda e: e.tensor_copy(goffb[:, :], PB(6)[:, 0:E_EXP]),
                 reads=[PB(6)[:, 0:E_EXP]], writes=[goffb[:, :]])
            posg = TB.view(21696, [128, NBLK], F32)
            accb = TB.view(21728, [128, NBLK], F32)
            for blk in range(NBLK):
                k.op("v", lambda e, blk=blk: e.tensor_mul(
                    selb[:, :], onehot[:, blk * E_EXP:(blk + 1) * E_EXP], goffb[:, :]),
                    reads=[onehot[:, blk * E_EXP:(blk + 1) * E_EXP], goffb[:, :]], writes=[selb[:, :]])
                k.op("v", lambda e: e.reduce_sum(scr_a[:, 0:1], selb[:, :], AX.X),
                     reads=[selb[:, :]], writes=[scr_a[:, 0:1]])
                k.op("v", lambda e, blk=blk: e.tensor_add(
                    scr_a[:, 1:2], scr_a[:, 0:1], poss[:, blk:blk + 1]),
                    reads=[scr_a[:, 0:1], poss[:, blk:blk + 1]],
                    writes=[scr_a[:, 1:2]])
                k.op("v", lambda e: e.tensor_scalar(
                    scr_a[:, 2:3], scr_a[:, 1:2], float(CAP), None, ALU.is_lt),
                    reads=[scr_a[:, 1:2]], writes=[scr_a[:, 2:3]])
                k.op("v", lambda e, blk=blk: e.tensor_copy(
                    posg[:, blk:blk + 1], scr_a[:, 1:2]),
                    reads=[scr_a[:, 1:2]], writes=[posg[:, blk:blk + 1]])
                k.op("v", lambda e, blk=blk: e.tensor_copy(
                    accb[:, blk:blk + 1], scr_a[:, 2:3]),
                    reads=[scr_a[:, 2:3]], writes=[accb[:, blk:blk + 1]])
                k.op("v", lambda e, blk=blk: e.tensor_scalar(
                    scr_a[:, 3:4], top1[:, blk:blk + 1], float(CAP), scr_a[:, 1:2],
                    ALU.mult, ALU.add),
                    reads=[top1[:, blk:blk + 1], scr_a[:, 1:2]],
                    writes=[scr_a[:, 3:4]])
                k.op("v", lambda e: e.tensor_mul(
                    scr_b[:, 1:2], scr_a[:, 3:4], scr_a[:, 2:3]),
                    reads=[scr_a[:, 2:4]], writes=[scr_b[:, 1:2]])
                k.op("v", lambda e: e.tensor_scalar(
                    scr_b[:, 2:3], scr_a[:, 2:3], -1.0, 1.0, ALU.mult, ALU.add),
                    reads=[scr_a[:, 2:3]], writes=[scr_b[:, 2:3]])
                k.op("v", lambda e: e.tensor_mul(
                    scr_b[:, 2:3], scr_b[:, 2:3], trashc[:, :]),
                    reads=[scr_b[:, 2:3], trashc[:, :]], writes=[scr_b[:, 2:3]])
                k.op("v", lambda e: e.tensor_add(
                    scr_b[:, 1:2], scr_b[:, 1:2], scr_b[:, 2:3]),
                    reads=[scr_b[:, 1:3]], writes=[scr_b[:, 1:2]])
                k.op("v", lambda e, blk=blk: e.tensor_copy(
                    slotu[:, blk:blk + 1], scr_b[:, 1:2]),
                    reads=[scr_b[:, 1:2]], writes=[slotu[:, blk:blk + 1]])
            invd_pa = invd.rearrange("(p a) -> p a", p=128)
            invd_rows = invd.rearrange("(a one) -> a one", one=1)
            for z0 in range(0, INVROWS, 128):
                zn = min(128, INVROWS - z0)
                k.dma("g", invd_pa[:, z0:z0 + zn], zerot[:, 0:zn], "dsc")
            for blk in range(NBLK):
                k.op("g", lambda e, blk=blk: e.indirect_dma_start(
                    invd_rows,
                    IndirectOffsetOnAxis(ap=slotu[:, blk:blk + 1], axis=0),
                    tokid_sb[:, blk:blk + 1], None),
                    reads=[tokid_sb[:, blk:blk + 1], slotu[:, blk:blk + 1]],
                    writes=[invd_rows], dma=True, dma_chain="dsc%d" % (blk % 4))
            k.op("g", lambda e: e.collective_compute(
                "AllReduce", ALU.add, RG, [invd[:]], [invr[:]]),
                reads=[invd[:]], writes=[invr[:]], chain="icc1", serial=True)
            k.op("g", lambda e: e.load_library(library_config.mlp),
                 chain="lib", serial=True)
            invmy = TB.view(8192, [128, 1, 128], F32)
            invr_rows = invr.rearrange("(r c) -> r c", c=128)
            k.op("g", lambda e: e.dma_gather(
                invmy, invr_rows, myslots_sb[:, :], NSB, NSB, 128),
                reads=[invr_rows, myslots_sb[:, :]], writes=[invmy],
                dma=True, dma_chain="dsc")
            invmy2 = TB.view(8192, [128, 128], F32)
            k.op("t", lambda e: e.transpose(
                ptf[0][:, 0:NSB], invmy2[0:NSB, :], identf[0:NSB, 0:NSB]),
                reads=[invmy2, identf[:, :]], writes=[ptf[0]], sig=("k", 128, 0))
            k.op("v", lambda e: e.tensor_copy(invT[:, 0:NSB], ptf[0][:, 0:NSB]),
                 reads=[ptf[0]], writes=[invT[:, 0:NSB]])

            bufg = [TB.view(o, [128, D], BF16)
                    for o in (0, 512, 8320, 8832, 9344, 9856, 10368, 10880,
                              13440, 13952, 14464, 14976, 15488, 16000,
                              16512, 17024)]
            buffm = [TB.view(1024, [128, 8, 512], BF16),
                     TB.view(11392, [128, 8, 512], BF16)]
            gfm = TB.view(3072, [128, 8, 512], BF16)
            ufm = TB.view(5120, [128, 8, 512], BF16)
            ystage = [TB.view(7168, [128, D], BF16), TB.view(7680, [128, D], BF16)]
            wslot = [WA.view(4096 * i, [128, 8, D], BF16) for i in range(4)]

            def load_mat(slot, src, e_loc, chain):
                for kk in range(8):
                    k.dma("g", wslot[slot][:, kk, :],
                          src[e_loc, kk * 128:(kk + 1) * 128, :], chain)

            groups = []
            for e_loc in range(EPC):
                b0 = e_loc * BPE
                rem = BPE
                while rem > 0:
                    n = min(4, rem)
                    groups.append((e_loc, b0 + (BPE - rem), n))
                    rem -= n
            load_mat(0, wgate_in, 0, "dw0")
            load_mat(1, wup_in, 0, "dw1")
            load_mat(2, wdown_in, 0, "dw0")
            nextslot = 3
            loaded_e = 0

            def emit_gathers(gi):
                _, sb0_g, n_g = groups[gi]
                for j in range(n_g):
                    sb = sb0_g + j
                    bg = bufg[sb % 16]
                    k.op("g", lambda e, bg=bg, sb=sb: e.indirect_dma_start(
                        bg, None, h2ag[:, :],
                        IndirectOffsetOnAxis(ap=invT[:, sb:sb + 1], axis=0)),
                        reads=[h2ag[:, :], invT[:, sb:sb + 1]], writes=[bg],
                        dma=True, dma_chain="dgh%d" % (sb % 4))

            gq = [0]

            def pump_gathers(upto):
                while gq[0] < min(upto, len(groups)):
                    emit_gathers(gq[0])
                    gq[0] += 1

            pump_gathers(1)
            yag_next = [0]
            for gi, (e_loc, sb0, n) in enumerate(groups):
                sg = wslot[(3 * e_loc) % 4]
                su = wslot[(3 * e_loc + 1) % 4]
                sd = wslot[(3 * e_loc + 2) % 4]
                first_of_e = (gi == 0 or groups[gi - 1][0] != e_loc)
                last_of_e = (gi + 1 == len(groups) or groups[gi + 1][0] != e_loc)
                if first_of_e and e_loc + 1 < EPC:
                    load_mat((3 * (e_loc + 1)) % 4, wgate_in, e_loc + 1, "dw1")
                pump_gathers(gi + 4)
                bfm = buffm[gi % 2]
                N = n * 128
                for j in range(n):
                    sb = sb0 + j
                    bg = bufg[sb % 16]
                    for f in range(8):
                        pt = ptp[f % 4]
                        k.op("t", lambda e, bg=bg, f=f, pt=pt: e.transpose(
                            pt, bg[:, f * 128:(f + 1) * 128], identb[:, :]),
                            reads=[bg, identb[:, :]], writes=[pt], sig=("k", 128, 0))
                        dst = bfm[:, f, j * 128:(j + 1) * 128]
                        eng = "v" if f % 2 else "s"
                        k.op(eng, lambda e, dst=dst, pt=pt: (
                            e.tensor_copy(dst, pt) if hasattr(e, "tensor_copy")
                            else e.copy(dst, pt)),
                            reads=[pt], writes=[dst])
                for m in range(8):
                    pq = PB(m % 4)
                    for kk in range(8):
                        k.op("t", lambda e, m=m, kk=kk, pq=pq, sg=sg, N=N, bfm=bfm: e.matmul(
                            pq[:, 0:N], sg[:, kk, m * 128:(m + 1) * 128],
                            bfm[:, kk, 0:N], start=(kk == 0), stop=(kk == 7)),
                            reads=[sg[:, kk, :], bfm[:, kk, 0:N]],
                            writes=[pq[:, 0:N]], inc=(kk == 7), sig=("k", 128, 0))
                    k.op("s", lambda e, m=m, pq=pq, N=N: e.activation(
                        gfm[:, m, 0:N], pq[:, 0:N], AF.Silu),
                        reads=[pq[:, 0:N]], writes=[gfm[:, m, 0:N]])
                if last_of_e and e_loc + 1 < EPC:
                    load_mat((3 * (e_loc + 1) + 1) % 4, wup_in, e_loc + 1, "dw0")
                for m in range(8):
                    pq = PB(m % 4)
                    for kk in range(8):
                        k.op("t", lambda e, m=m, kk=kk, pq=pq, su=su, N=N, bfm=bfm: e.matmul(
                            pq[:, 0:N], su[:, kk, m * 128:(m + 1) * 128],
                            bfm[:, kk, 0:N], start=(kk == 0), stop=(kk == 7)),
                            reads=[su[:, kk, :], bfm[:, kk, 0:N]],
                            writes=[pq[:, 0:N]], inc=(kk == 7), sig=("k", 128, 0))
                    k.op("v", lambda e, m=m, pq=pq, N=N: e.tensor_copy(
                        ufm[:, m, 0:N], pq[:, 0:N]),
                        reads=[pq[:, 0:N]], writes=[ufm[:, m, 0:N]])
                    k.op("v", lambda e, m=m, N=N: e.tensor_mul(
                        gfm[:, m, 0:N], gfm[:, m, 0:N], ufm[:, m, 0:N]),
                        reads=[gfm[:, m, 0:N], ufm[:, m, 0:N]],
                        writes=[gfm[:, m, 0:N]])
                if last_of_e and e_loc + 1 < EPC:
                    load_mat((3 * (e_loc + 1) + 2) % 4, wdown_in, e_loc + 1, "dw1")
                for ts in range(n):
                    py = PB(4, 2) if ts % 2 == 0 else PB(6, 2)
                    for nh in range(2):
                        for f in range(8):
                            k.op("t", lambda e, ts=ts, f=f, nh=nh, py=py, sd=sd:
                                 e.matmul(
                                     py[:, nh * 512:(nh + 1) * 512],
                                     gfm[:, f, ts * 128:(ts + 1) * 128],
                                     sd[:, f, nh * 512:(nh + 1) * 512],
                                     start=(f == 0), stop=(f == 7)),
                                 reads=[gfm[:, f, :], sd[:, f, :]],
                                 writes=[py[:, nh * 512:(nh + 1) * 512]],
                                 inc=(f == 7), sig=("k", 128, 0))
                    ys = ystage[ts % 2]
                    k.op("v", lambda e, ys=ys, py=py: e.tensor_copy(ys, py),
                         reads=[py], writes=[ys])
                    k.dma("g", yin[(sb0 + ts) * 128:(sb0 + ts + 1) * 128, :], ys,
                          "dys")
                stored_e = (sb0 + n - e_loc * BPE) * 128
                while yag_next[0] < 2 * EPC:
                    e_ag, hf = yag_next[0] // 2, yag_next[0] % 2
                    ready = e_ag < e_loc or (
                        e_ag == e_loc and (hf + 1) * HCAP <= stored_e)
                    if not ready:
                        break
                    i0 = e_ag * CAP + hf * HCAP
                    o0 = (e_ag * 2 + hf) * n_cores * HCAP
                    k.op("g", lambda e_, i0=i0, o0=o0: e_.collective_compute(
                        "AllGather", ALU.bypass, RG,
                        [yin[i0:i0 + HCAP, :]],
                        [yag[o0:o0 + n_cores * HCAP, :]]),
                        reads=[yin[i0:i0 + HCAP, :]],
                        writes=[yag[o0:o0 + n_cores * HCAP, :]],
                        chain="ycc%d" % (e_ag * 2 + hf), serial=True)
                    yag_next[0] += 1

            for blk in range(NBLK):
                k.op("v", lambda e, blk=blk: e.tensor_mul(
                    gatem[:, blk:blk + 1], gatem[:, blk:blk + 1],
                    accb[:, blk:blk + 1]),
                    reads=[gatem[:, blk:blk + 1], accb[:, blk:blk + 1]],
                    writes=[gatem[:, blk:blk + 1]])
                k.op("v", lambda e, blk=blk: e.tensor_copy(
                    scr_i[:, 8:9], top1[:, blk:blk + 1]),
                    reads=[top1[:, blk:blk + 1]], writes=[scr_i[:, 8:9]])
                k.op("v", lambda e: e.tensor_scalar(
                    scr_i[:, 9:10], scr_i[:, 8:9], EPC - 1, None,
                    ALU.bitwise_and),
                    reads=[scr_i[:, 8:9]], writes=[scr_i[:, 9:10]])
                k.op("v", lambda e: e.tensor_copy(scr_b[:, 4:5], scr_i[:, 9:10]),
                     reads=[scr_i[:, 9:10]], writes=[scr_b[:, 4:5]])
                k.op("v", lambda e, blk=blk: e.tensor_scalar(
                    scr_b[:, 5:6], top1[:, blk:blk + 1], scr_b[:, 4:5],
                    1.0 / EPC, ALU.subtract, ALU.mult),
                    reads=[top1[:, blk:blk + 1], scr_b[:, 4:5]],
                    writes=[scr_b[:, 5:6]])
                k.op("v", lambda e, blk=blk: e.tensor_scalar(
                    scr_b[:, 9:10], posg[:, blk:blk + 1], float(HCAP), None,
                    ALU.is_ge),
                    reads=[posg[:, blk:blk + 1]], writes=[scr_b[:, 9:10]])
                k.op("v", lambda e, blk=blk: e.tensor_scalar(
                    scr_b[:, 10:11], scr_b[:, 9:10], -float(HCAP),
                    posg[:, blk:blk + 1], ALU.mult, ALU.add),
                    reads=[scr_b[:, 9:10], posg[:, blk:blk + 1]],
                    writes=[scr_b[:, 10:11]])
                k.op("v", lambda e: e.tensor_scalar(
                    scr_b[:, 6:7], scr_b[:, 9:10], float(n_cores * HCAP),
                    scr_b[:, 10:11], ALU.mult, ALU.add),
                    reads=[scr_b[:, 9:11]], writes=[scr_b[:, 6:7]])
                k.op("v", lambda e: e.tensor_scalar(
                    scr_b[:, 7:8], scr_b[:, 4:5], float(2 * n_cores * HCAP),
                    scr_b[:, 6:7], ALU.mult, ALU.add),
                    reads=[scr_b[:, 4:7]], writes=[scr_b[:, 7:8]])
                k.op("v", lambda e: e.tensor_scalar(
                    scr_b[:, 7:8], scr_b[:, 5:6], float(HCAP), scr_b[:, 7:8],
                    ALU.mult, ALU.add),
                    reads=[scr_b[:, 5:8]], writes=[scr_b[:, 7:8]])
                k.op("v", lambda e, blk=blk: e.tensor_mul(
                    scr_b[:, 7:8], scr_b[:, 7:8], accb[:, blk:blk + 1]),
                    reads=[scr_b[:, 7:8], accb[:, blk:blk + 1]],
                    writes=[scr_b[:, 7:8]])
                k.op("v", lambda e, blk=blk: e.tensor_scalar(
                    scr_b[:, 8:9], accb[:, blk:blk + 1], -float(GRID - 1),
                    float(GRID - 1), ALU.mult, ALU.add),
                    reads=[accb[:, blk:blk + 1]], writes=[scr_b[:, 8:9]])
                k.op("v", lambda e: e.tensor_add(
                    scr_b[:, 7:8], scr_b[:, 7:8], scr_b[:, 8:9]),
                    reads=[scr_b[:, 7:9]], writes=[scr_b[:, 7:8]])
                k.op("v", lambda e, blk=blk: e.tensor_copy(
                    slotcl[:, blk:blk + 1], scr_b[:, 7:8]),
                    reads=[scr_b[:, 7:8]], writes=[slotcl[:, blk:blk + 1]])

        if phase_limit >= 4:
            yg = [TB.view(13440 + 512 * i, [128, D], BF16) for i in range(4)]
            x1r = [TB.view(15488 + 1024 * i, [128, D]) for i in range(4)]
            h2f = [TB.view(19584, [128, D]), TB.view(20608, [128, D]),
                   TB.view(0, [128, D]), TB.view(1024, [128, D])]
            fbuf = [TB.view(2048, [128, D]), TB.view(3072, [128, D])]
            tbufF = [TB.view(4096, [128, D]), TB.view(5120, [128, D])]
            outb = [TB.view(6144 + 1024 * i, [128, D]) for i in range(4)]
            def finpre(blk):
                p4 = blk % 4
                k.op("g", lambda e, p4=p4, blk=blk: e.indirect_dma_start(
                    yg[p4], None, yag[:, :],
                    IndirectOffsetOnAxis(ap=slotcl[:, blk:blk + 1], axis=0)),
                    reads=[yag[:, :], slotcl[:, blk:blk + 1]], writes=[yg[p4]],
                    dma=True, dma_chain="dyg%d" % p4)
                k.dma("q", x1r[p4], x1d[blk * 128:(blk + 1) * 128, :],
                      "dxf%d" % p4)
                k.op("s", lambda e, p4=p4, blk=blk: e.activation(
                    h2f[p4], x1r[p4], AF.Identity,
                    bias=m2r2[:, 2 * blk:2 * blk + 1],
                    scale=m2r2[:, 2 * blk + 1:2 * blk + 2]),
                    reads=[x1r[p4], m2r2[:, 2 * blk:2 * blk + 1], m2r2[:, 2 * blk + 1:2 * blk + 2]],
                    writes=[h2f[p4]])

            def fin_c2(blk):
                p4 = blk % 4
                p2 = blk % 2
                k.op("v", lambda e, p4=p4, p2=p2: e.tensor_add(
                    outb[p4], x1r[p4], tbufF[p2]),
                    reads=[x1r[p4], tbufF[p2]], writes=[outb[p4]])
                k.dma("g", y_out[blk * 128:(blk + 1) * 128, :], outb[p4],
                      "dout%d" % p2)

            finpre(0)
            finpre(1)
            for blk in range(NBLK):
                p4 = blk % 4
                p2 = blk % 2
                if blk + 2 < NBLK:
                    finpre(blk + 2)
                k.op("v", lambda e, p4=p4, p2=p2, blk=blk: e.scalar_tensor_tensor(
                    fbuf[p2], yg[p4], gatem[:, blk:blk + 1], h2f[p4],
                    ALU.mult, ALU.add),
                    reads=[yg[p4], gatem[:, blk:blk + 1], h2f[p4]],
                    writes=[fbuf[p2]])
                ln_stats(fbuf[p2], scr_a[:, 8 + 2 * p2:9 + 2 * p2],
                         scr_a[:, 9 + 2 * p2:10 + 2 * p2])
                k.op("s", lambda e, p2=p2: e.activation(
                    tbufF[p2], fbuf[p2], AF.Identity,
                    bias=scr_a[:, 8 + 2 * p2:9 + 2 * p2],
                    scale=scr_a[:, 9 + 2 * p2:10 + 2 * p2]),
                    reads=[fbuf[p2], scr_a[:, 8 + 2 * p2:10 + 2 * p2]],
                    writes=[tbufF[p2]])
                if blk >= 1:
                    fin_c2(blk - 1)
            fin_c2(NBLK - 1)
            k.wait("g", "dout0", k.barrier_value("dout0"))
            k.wait("g", "dout1", k.barrier_value("dout1"))

        with nc.Block() as block:
            k.finalize(block)
    mybir.codegen_inst_isa_subclasses(nc)
    return nc



def _stage_inputs(inputs, n_cores=8, cap=None):
    L = inputs["L0_attn_hidden_states"].shape[1]
    NBLK = L // 128
    T_tot = n_cores * L
    CAP = cap if cap is not None else int(math.ceil(CF * T_tot / E_EXP))
    EPC = E_EXP // n_cores
    NSB = EPC * CAP // 128
    x = np.ascontiguousarray(np.asarray(inputs["L0_attn_hidden_states"],
                                        dtype=np.float32))
    com = {
        "wq": np.ascontiguousarray(inputs["L0_attn_Wq"], dtype=np.float32),
        "wk": np.ascontiguousarray(inputs["L0_attn_Wk"], dtype=np.float32),
        "wv": np.ascontiguousarray(inputs["L0_attn_Wv"], dtype=np.float32),
        "wg": np.ascontiguousarray(inputs["L0_attn_Wg"], dtype=np.float32),
        "wgk1": np.ascontiguousarray(inputs["L0_attn_Wgk1"], dtype=np.float32),
        "wgk2": np.ascontiguousarray(inputs["L0_attn_Wgk2"], dtype=np.float32),
        "bgk2": np.ascontiguousarray(
            np.asarray(inputs["L0_attn_bgk2"], dtype=np.float32).reshape(1, -1)),
        "gnw": np.ascontiguousarray(
            np.asarray(inputs["L0_attn_g_norm_w"], dtype=np.float32).reshape(1, -1)),
        "wo": np.ascontiguousarray(inputs["L0_attn_Wo"], dtype=np.float32),
        "wr": np.ascontiguousarray(inputs["L0_ffn_w_router"], dtype=np.float32),
    }
    wgate = np.ascontiguousarray(inputs["L0_ffn_w_gate"], dtype=np.float32)
    wup = np.ascontiguousarray(inputs["L0_ffn_w_up"], dtype=np.float32)
    wdown = np.ascontiguousarray(inputs["L0_ffn_w_down"], dtype=np.float32)
    in_maps = []
    for c in range(n_cores):
        m = dict(com)
        m["x"] = x[c]
        m["wgate"] = wgate[c * EPC:(c + 1) * EPC]
        m["wup"] = wup[c * EPC:(c + 1) * EPC]
        m["wdown"] = wdown[c * EPC:(c + 1) * EPC]
        cm = np.zeros((8, 1), dtype=np.float32)
        cm[:c] = 1.0
        m["coremask"] = cm
        blks = np.arange(NBLK)[None, :]
        tok = ((blks // 4) * (n_cores * 512) + c * 512 + (blks % 4) * 128
               + np.arange(128)[:, None]).astype(np.float32)
        m["tokid"] = np.ascontiguousarray(tok)
        ncol = -(-NSB // 16)
        ms = np.full((128, ncol), -1, dtype=np.int16)
        base = c * NSB
        for i in range(NSB):
            for g in range(8):
                ms[i % 16 + 16 * g, i // 16] = base + i
        m["myslots"] = ms
        in_maps.append(m)
    return in_maps


def kernel(**inputs):
    n_cores = 8
    L = inputs["L0_attn_hidden_states"].shape[1]
    in_maps = _stage_inputs(inputs, n_cores)
    nc = build_core(L=L, n_cores=n_cores)
    res = run_bass_kernel_spmd(nc, in_maps, core_ids=list(range(n_cores)))
    return np.stack([res.results[c]["y"] for c in range(n_cores)], axis=0)
